# version 03
# baseline (speedup 1.0000x reference)
import numpy as np
from contextlib import ExitStack

import concourse.bass as bass
import concourse.mybir as mybir
from concourse.bass_utils import run_bass_kernel_spmd

F32 = mybir.dt.float32
BF16 = mybir.dt.bfloat16
AF = mybir.ActivationFunctionType
ALU = mybir.AluOpType
AX = mybir.AxisListType

NCORES = 8
SEQ = 4
T = 2048
NT = 16
D = 1024
NDC = 8
K = 256
KC = 2
HID = 2816
NHB = 22
E = 8
NSW = 2
NTT = 8
LN2C = 0.6931471805599453

DEBUG_STAGE = "full"
PREP_ON = True
TRUNC_QB = 4
CUTS = set()
DUMP = False


def build_program():
    nc = bass.Bass("TRN2", target_bir_lowering=False, debug=False,
                   dynamic_dma_scratch_size=16384,
                   detect_race_conditions=False)

    x_d = nc.dram_tensor("x", [SEQ, T, D], F32, kind="ExternalInput").ap()
    xT_d = nc.dram_tensor("xT", [SEQ, D, T], F32, kind="ExternalInput").ap()
    cl_d = nc.dram_tensor("cl", [D, K], F32, kind="ExternalInput").ap()
    cr_d = nc.dram_tensor("cr", [D, K], F32, kind="ExternalInput").ap()
    thr_d = nc.dram_tensor("thr", [1, 1], F32, kind="ExternalInput").ap()
    gb_d = nc.dram_tensor("gb", [1, 1], F32, kind="ExternalInput").ap()
    al_d = nc.dram_tensor("alpha", [1, 1], F32, kind="ExternalInput").ap()
    nw_d = nc.dram_tensor("nw", [128, NDC], F32, kind="ExternalInput").ap()
    rtrT_d = nc.dram_tensor("rtrT", [D, E], F32, kind="ExternalInput").ap()
    wguT_d = nc.dram_tensor("wguT", [E, D, 2 * HID], F32, kind="ExternalInput").ap()
    wdnT_d = nc.dram_tensor("wdnT", [E, HID, D], F32, kind="ExternalInput").ap()
    out_d = nc.dram_tensor("out", [SEQ, T, D], F32, kind="ExternalOutput").ap()

    if DUMP:
        d_xfm = nc.dram_tensor("d_xfm", [128, NDC * 2048], BF16, kind="ExternalOutput").ap()
        d_lrt = nc.dram_tensor("d_lrt", [128, 8192], BF16, kind="ExternalOutput").ap()
        d_pt = nc.dram_tensor("d_pt", [128, 8192], BF16, kind="ExternalOutput").ap()
        d_st = nc.dram_tensor("d_st", [128, 64], F32, kind="ExternalOutput").ap()
        d_t0 = nc.dram_tensor("d_t0", [128, 1024], F32, kind="ExternalOutput").ap()
        d_s0 = nc.dram_tensor("d_s0", [128, 16], F32, kind="ExternalOutput").ap()
    sguT = nc.dram_tensor("sguT", [E, NDC, 128, 2 * HID], BF16, kind="Internal").ap()
    sdnT = nc.dram_tensor("sdnT", [E, HID, D], BF16, kind="Internal").ap()
    tokscr = nc.dram_tensor("tokscr", [T, D], BF16, kind="Internal").ap()

    ops = {e: [] for e in ("sp", "pe", "dve", "act", "pool")}

    def em(eng, f):
        ops[eng].append(f)

    cnt = {"sp": 0, "pl": 0, "pe": 0, "dv": 0, "ac": 0, "st": 0}

    with ExitStack() as ctx:
        def sbuf(name, shape, dt):
            return ctx.enter_context(nc.sbuf_tensor(name, shape, dt))

        guard = ctx.enter_context(nc.sbuf_tensor("topguard", [128, 4096], mybir.dt.uint8, side="right"))
        cab = sbuf("cab", [128, NT * E], F32)
        stats = sbuf("stats", [128, 32], F32)
        misc = sbuf("misc", [128, 16], F32)
        wcols = sbuf("wcols", [128, NDC], F32)
        rtrTf = sbuf("rtrTf", [128, NDC * E], F32)
        rtrTb = sbuf("rtrTb", [128, NDC * E], BF16)
        e8 = sbuf("e8", [128, 4 * E], F32)
        sc_sb = sbuf("sc_sb", [1, 4], F32)
        ones_sb = sbuf("ones_sb", [1, 128], F32)
        sq = sbuf("sq", [128, 512], F32)
        silu_sb = sbuf("silu_sb", [128, 512], BF16)
        silu2_sb = sbuf("silu2_sb", [128, 512], BF16)
        sgex = sbuf("sgex", [128, 1024], BF16)
        stgA = sbuf("stgA", [128, 1408], BF16)
        stream = sbuf("stream", [128, NT * 1024], F32)
        xfm = sbuf("xfm", [128, NDC * 2048], BF16)
        regB = sbuf("regB", [128, NHB * 1024], BF16)
        regC = sbuf("regC", [128, NHB * 1024], BF16)
        regD = sbuf("regD", [128, 4096], BF16)

        PA = ctx.enter_context(nc.psum_tensor("PA", [128, 2048], F32))
        PB = ctx.enter_context(nc.psum_tensor("PB", [128, 2048], F32))

        s_sp = ctx.enter_context(nc.semaphore("s_sp"))
        s_pl = ctx.enter_context(nc.semaphore("s_pl"))
        s_pe = ctx.enter_context(nc.semaphore("s_pe"))
        s_dv = ctx.enter_context(nc.semaphore("s_dv"))
        s_ac = ctx.enter_context(nc.semaphore("s_ac"))
        s_st = ctx.enter_context(nc.semaphore("s_st"))
        SEM = {"sp": s_sp, "pl": s_pl, "pe": s_pe, "dv": s_dv, "ac": s_ac, "st": s_st}

        def dma(eng, out, in_, sem, transpose=False):
            cnt[sem] += 16
            s = SEM[sem]
            em(eng, lambda h, o=out, i=in_, s=s, tr=transpose:
               h.dma_start(out=o, in_=i, transpose=tr).then_inc(s, 16))
            return cnt[sem]

        def wait(eng, sem, v):
            if v <= 0:
                return
            s = SEM[sem]
            em(eng, lambda h, s=s, v=v: h.wait_ge(s, v))

        def bump(sem, n=1):
            cnt[sem] += n
            return cnt[sem]

        def dvg():
            em("dve", lambda h: h.tensor_scalar_mul(misc[0:128, 9:10], misc[0:128, 5:6], 0.0))

        def mm(pslc, lhs, rhs, start, stop, inc=False):
            if inc:
                v = bump("pe")
                em("pe", lambda h, p=pslc, l=lhs, r=rhs, a=start, b=stop:
                   nc.tensor.matmul(p, l, r, start=a, stop=b).then_inc(s_pe, 1))
                return v
            em("pe", lambda h, p=pslc, l=lhs, r=rhs, a=start, b=stop:
               nc.tensor.matmul(p, l, r, start=a, stop=b))
            return None

        THR, GB, ALC, NTHR, GLN2, ZERO, EPS5, EPS6 = range(8)

        def mcol(i):
            return misc[0:128, i:i + 1]

        ones_d = nc.inline_tensor(np.ones((1, 128), dtype=np.float32), "onesrow")
        dma("sp", sc_sb[0:1, 0:1], thr_d, "sp")
        dma("sp", sc_sb[0:1, 1:2], gb_d, "sp")
        dma("sp", sc_sb[0:1, 2:3], al_d, "sp")
        dma("sp", wcols[:, :], nw_d, "sp")
        dma("sp", rtrTf[:, :].rearrange("p (dc e) -> p dc e", e=E),
            rtrT_d.rearrange("(dc p) e -> p dc e", p=128), "sp")
        v_pro = dma("sp", ones_sb[0:1, :], ones_d.ap(), "sp")
        if "pro" in CUTS:
            prologue_dv = 0
        else:
            wait("pe", "sp", v_pro)
        if "pro" not in CUTS:
            v = mm(PA[:, 0:3], ones_sb[0:1, :], sc_sb[0:1, 0:3], True, True, inc=True)
            wait("dve", "sp", v_pro)
            wait("dve", "pe", v)
            em("dve", lambda h: h.tensor_copy(misc[:, 0:3], PA[:, 0:3]))
            em("dve", lambda h: h.memset(mcol(ZERO), 0.0))
            em("dve", lambda h: h.tensor_scalar(mcol(ALC), mcol(ALC), 2.0, -0.5, ALU.min, ALU.max))
            em("dve", lambda h: h.tensor_scalar_mul(mcol(NTHR), mcol(THR), -1.0))
            em("dve", lambda h: h.tensor_scalar_add(mcol(GLN2), mcol(GB), -LN2C))
            em("dve", lambda h: h.memset(mcol(EPS5), 1e-5))
            em("dve", lambda h: h.memset(misc[0:128, 8:9], 1.0))
            em("dve", lambda h: h.memset(mcol(EPS6), 1e-6))
            prologue_dv = bump("dv")
            em("dve", lambda h: h.tensor_copy(rtrTb[:, :], rtrTf[:, :]).then_inc(s_dv, 1))
            wait("act", "dv", prologue_dv)

        m_gu = [0] * E
        m_dn = [0] * E
        wguT_r = wguT_d.rearrange("e (dc p) h -> e dc p h", p=128)
        wdnT_r2 = wdnT_d.rearrange("e (c p) d -> e p c d", p=128)
        sdnT_r2 = sdnT.rearrange("e (c p) d -> e p c d", p=128)
        last_store = 0
        for e in range(E if PREP_ON else 0):
            for dc in range(NDC):
                for hf in range(2):
                    for c0 in (0, 1408):
                        wait("pool", "pl", last_store)
                        vc = dma("pool", stgA[:, :],
                                 wguT_r[e, dc, :, hf * HID + c0:hf * HID + c0 + 1408], "pl")
                        wait("pool", "pl", vc)
                        last_store = dma("pool", sguT[e, dc, :, hf * HID + c0:hf * HID + c0 + 1408],
                                         stgA[:, :], "pl")
            m_gu[e] = last_store
            for c2 in range(NHB):
                wait("pool", "pl", last_store)
                vc = dma("pool", stgA[:, 0:1024], wdnT_r2[e, :, c2, :], "pl")
                wait("pool", "pl", vc)
                last_store = dma("pool", sdnT_r2[e, :, c2, :], stgA[:, 0:1024], "pl")
            m_dn[e] = last_store

        x_tok = regB[:, 0:NT * 1024]
        e_sb = regB[:, NT * 1024:NT * 1024 + 2048]
        hidden = regB[:, 0:NHB * 1024]
        LT = regC[:, 0:4096]
        RT = regC[:, 4096:8192]
        CLb = regC[:, 8192:10240]
        CRb = regC[:, 10240:12288]
        PT = regC[:, 12288:20480]
        wd_sb = regC[:, 0:NHB * 1024]

        out_r = out_d.rearrange("s (tc p) d -> s p tc d", p=128)
        x_r = x_d.rearrange("s (tc p) d -> s p tc d", p=128)
        tok_r = tokscr.rearrange("(tc p) d -> p tc d", p=128)
        stream_r = stream[:, :].rearrange("p (tc d) -> p tc d", d=1024)
        xtok_r = x_tok.rearrange("p (tc d) -> p tc d", d=1024)
        sdnT_r = sdnT.rearrange("e (hc p) d -> e p hc d", p=128)
        sguT_p = sguT.rearrange("e dc p h -> e p dc h")

        pe_ffnA_done = 0
        pe_ffnB_done = 0
        dv_seq_done = prologue_dv
        st_prev = 0
        v_nf_prev = 0

        def ln1(t):
            stt = stream[:, t * 1024:(t + 1) * 1024]
            a = (t % 2) * 16
            ns_c = stats[0:128, a + 0:a + 1]
            nm_c = stats[0:128, a + 1:a + 2]
            vs_c = stats[0:128, a + 2:a + 3]
            sd_c = stats[0:128, a + 3:a + 4]
            rs_c = stats[0:128, a + 4:a + 5]
            vs2_c = stats[0:128, a + 14:a + 15]
            em("dve", lambda h, i=stt, o=ns_c: h.reduce_sum(o, i, axis=AX.X, negate=True))
            dvg()
            v = bump("dv")
            em("dve", lambda h, i=ns_c, o=nm_c: h.tensor_scalar_mul(o, i, 1.0 / D).then_inc(s_dv, 1))
            wait("act", "dv", v)
            em("act", lambda h, i=stt, b=nm_c, ao=vs_c: nc.scalar.activation(
                sq[:, :], i[:, 0:512], AF.Square, bias=b, scale=1.0, accum_out=ao))
            va = bump("ac")
            em("act", lambda h, i=stt, b=nm_c, ao=vs2_c: nc.scalar.activation(
                sq[:, :], i[:, 512:1024], AF.Square, bias=b, scale=1.0, accum_out=ao).then_inc(s_ac, 1))
            wait("dve", "ac", va)
            v = bump("dv")
            em("dve", lambda h, o=vs_c, i=vs_c, j=vs2_c: h.tensor_tensor(o, i, j, ALU.add).then_inc(s_dv, 1))
            dvg()
            wait("act", "dv", v)
            va = bump("ac")
            em("act", lambda h, i=vs_c, o=sd_c: nc.scalar.activation(
                o, i, AF.Sqrt, bias=mcol(EPS5), scale=1.0 / D).then_inc(s_ac, 1))
            wait("dve", "ac", va)
            em("dve", lambda h, i=sd_c, o=rs_c: h.reciprocal(o, i))
            dvg()
            v = bump("dv")
            em("dve", lambda h, io=stt, b=nm_c, r=rs_c: h.tensor_scalar(
                io, io, b, r, ALU.add, ALU.mult).then_inc(s_dv, 1))
            return v

        def rms(t, dv_ready):
            stt = stream[:, t * 1024:(t + 1) * 1024]
            wait("act", "dv", dv_ready)
            a = (t % 2) * 16
            ss_c = stats[0:128, a + 5:a + 6]
            s2_c = stats[0:128, a + 6:a + 7]
            r2_c = stats[0:128, a + 7:a + 8]
            ss2_c = stats[0:128, a + 15:a + 16]
            em("act", lambda h, i=stt, ao=ss_c: nc.scalar.activation(
                sq[:, :], i[:, 0:512], AF.Square, bias=mcol(ZERO), scale=1.0, accum_out=ao))
            va = bump("ac")
            em("act", lambda h, i=stt, ao=ss2_c: nc.scalar.activation(
                sq[:, :], i[:, 512:1024], AF.Square, bias=mcol(ZERO), scale=1.0, accum_out=ao).then_inc(s_ac, 1))
            wait("dve", "ac", va)
            v = bump("dv")
            em("dve", lambda h, o=ss_c, i=ss_c, j=ss2_c: h.tensor_tensor(o, i, j, ALU.add).then_inc(s_dv, 1))
            dvg()
            wait("act", "dv", v)
            va = bump("ac")
            em("act", lambda h, i=ss_c, o=s2_c: nc.scalar.activation(
                o, i, AF.Sqrt, bias=mcol(EPS6), scale=1.0 / D).then_inc(s_ac, 1))
            wait("dve", "ac", va)
            em("dve", lambda h, i=s2_c, o=r2_c: h.reciprocal(o, i))
            dvg()
            yn = x_tok[:, t * 1024:(t + 1) * 1024]
            v = bump("dv")
            em("dve", lambda h, i=stt, r=r2_c, o=yn: h.tensor_scalar_mul(
                o, i, r).then_inc(s_dv, 1))
            return v

        for s in range(SEQ):
            wait("sp", "st", st_prev)
            v_x = dma("sp", stream_r, x_r[s], "sp")
            v_ct = 0
            v_fm = 0
            if "casts" not in CUTS:
                if "c_xtok" not in CUTS:
                    wait("pool", "pe", pe_ffnB_done)
                    v_ct = dma("pool", xtok_r, x_r[s], "pl")
                if "c_xfm" not in CUTS:
                    wait("pool", "pe", pe_ffnA_done)
                    for dc in range(NDC):
                        dma("pool", xfm[:, dc * 2048:(dc + 1) * 2048], xT_d[s, dc * 128:(dc + 1) * 128, :], "pl")
                if "c_clcr" not in CUTS:
                    wait("pool", "pl", v_ct)
                    dma("pool", CLb.rearrange("p (dc k) -> p dc k", k=K),
                        cl_d.rearrange("(dc p) k -> p dc k", p=128), "pl")
                    v_fm = dma("pool", CRb.rearrange("p (dc k) -> p dc k", k=K),
                               cr_d.rearrange("(dc p) k -> p dc k", p=128), "pl")
                v_fm = max(v_fm, v_ct)

            wait("dve", "sp", v_x)

            if "ltrt" in CUTS:
                attn_evacs_ac = 0
                pe_ltrt = 0
            wait("pe", "pl", v_fm)
            wait("pe", "dv", dv_seq_done)
            ac_ev = 0
            for cb, dst, scl, pbank in (() if "ltrt" in CUTS else ((CLb, LT, 1.0 / 32.0, PA), (CRb, RT, 1.0, PB))):
                for kc in range(KC):
                    wait("pe", "ac", ac_ev)
                    vp = None
                    for jg in range(4):
                        pslc = pbank[:, jg * 512:(jg + 1) * 512]
                        for dc in range(NDC):
                            lhs = cb[:, dc * 256 + kc * 128:dc * 256 + kc * 128 + 128]
                            rhs = xfm[:, dc * 2048 + jg * 512:dc * 2048 + (jg + 1) * 512]
                            vp = mm(pslc, lhs, rhs, dc == 0, dc == NDC - 1,
                                    inc=(jg == 3 and dc == NDC - 1))
                    wait("act", "pe", vp)
                    for jg in range(4):
                        pslc = pbank[:, jg * 512:(jg + 1) * 512]
                        dslc = dst[:, kc * 2048 + jg * 512:kc * 2048 + (jg + 1) * 512]
                        lastj = (jg == 3)
                        if lastj:
                            ac_ev = bump("ac")
                        em("act", lambda h, d=dslc, p=pslc, sc=scl, lj=lastj: nc.scalar.activation(
                            d, p, AF.Copy, bias=0.0, scale=sc).then_inc(s_ac, 1)
                            if lj else nc.scalar.activation(d, p, AF.Copy, bias=0.0, scale=sc))
            if "ltrt" not in CUTS:
                attn_evacs_ac = ac_ev
                pe_ltrt = cnt["pe"]

            pe_pv_done = 0
            dv_res = [0] * NT
            ac_slot = [attn_evacs_ac, attn_evacs_ac]
            dv_sgex = 0
            ac_prev_stats = attn_evacs_ac
            ac_stats = [attn_evacs_ac, attn_evacs_ac]
            dv_scale = [0] * NT
            for qb in range(TRUNC_QB):
                dv_pt = 0
                for jc in range(16):
                    pslc = PB[:, 1024 + (jc % 2) * 512:1536 + (jc % 2) * 512]
                    wait("pe", "ac", ac_slot[jc % 2])
                    v_st = None
                    for kc in range(KC):
                        lhs = RT[:, kc * 2048 + jc * 128:kc * 2048 + jc * 128 + 128]
                        rhs = LT[:, kc * 2048 + qb * 512:kc * 2048 + (qb + 1) * 512]
                        v_st = mm(pslc, lhs, rhs, kc == 0, kc == KC - 1, inc=(kc == KC - 1))
                    wait("act", "pe", v_st)
                    wait("act", "dv", dv_sgex)
                    sg = sgex[:, 0:512]
                    ex = sgex[:, 512:1024]
                    em("act", lambda h, o=sg, p=pslc: nc.scalar.sign(o, p, bias=mcol(NTHR)))
                    ac_slot[jc % 2] = bump("ac")
                    em("act", lambda h, o=ex, p=pslc: nc.scalar.activation(
                        o, p, AF.Exp, bias=mcol(GLN2), scale=1.0).then_inc(s_ac, 1))
                    wait("dve", "ac", ac_slot[jc % 2])
                    wait("dve", "pe", pe_pv_done)
                    ptp = PT[:, jc * 512:(jc + 1) * 512]
                    dv_sgex = bump("dv")
                    dv_pt = dv_sgex
                    em("dve", lambda h, o=ptp, sg=sg, ex=ex: h.scalar_tensor_tensor(
                        o, sg, 1.0, ex, ALU.add, ALU.mult).then_inc(s_dv, 1))

                    if jc % 4 == 3:
                        t = qb * 4 + jc // 4
                        wait("pe", "ac", ac_prev_stats)
                        v_sc = None
                        for jg in range(4):
                            pslc2 = PA[:, jg * 512:(jg + 1) * 512]
                            for kc in range(KC):
                                lhs = LT[:, kc * 2048 + t * 128:kc * 2048 + t * 128 + 128]
                                rhs = RT[:, kc * 2048 + jg * 512:kc * 2048 + (jg + 1) * 512]
                                v_sc = mm(pslc2, lhs, rhs, kc == 0, kc == KC - 1,
                                          inc=(jg == 3 and kc == KC - 1))
                        a = (t % 2) * 16
                        nm = stats[0:128, a + 8:a + 9]
                        mt = stats[0:128, a + 9:a + 10]
                        eb = stats[0:128, a + 10:a + 11]
                        Zc = stats[0:128, a + 11:a + 12]
                        emc = stats[0:128, a + 12:a + 13]
                        zr = stats[0:128, a + 13:a + 14]
                        scl_c = e8[0:128, (t % 4):(t % 4) + 1]
                        wait("dve", "pe", v_sc)
                        wait("dve", "ac", ac_stats[t % 2])
                        em("dve", lambda h, o=nm: h.reduce_max(o, PA[:, :], axis=AX.X, negate=True))
                        dvg()
                        em("dve", lambda h, o=mt, i=nm: h.tensor_tensor(o, mcol(GB), i, ALU.subtract))
                        dvg()
                        v = bump("dv")
                        em("dve", lambda h, o=eb, i=mt: h.tensor_tensor(
                            o, mcol(GB), i, ALU.subtract).then_inc(s_dv, 1))
                        wait("act", "dv", v)
                        for bk in range(4):
                            em("act", lambda h, b=eb, bk=bk: nc.scalar.activation(
                                e_sb[:, bk * 512:(bk + 1) * 512],
                                PA[:, bk * 512:(bk + 1) * 512], AF.Exp, bias=b,
                                scale=1.0))
                        ac_prev_stats = bump("ac")
                        ac_stats[t % 2] = ac_prev_stats
                        em("act", lambda h, i=mt, o=emc: nc.scalar.activation(
                            o, i, AF.Exp, bias=mcol(ZERO), scale=-1.0).then_inc(s_ac, 1))
                        wait("dve", "ac", ac_prev_stats)
                        em("dve", lambda h, o=Zc: h.reduce_sum(o, e_sb, axis=AX.X))
                        dvg()
                        em("dve", lambda h, o=zr, i=Zc: h.reciprocal(o, i))
                        dvg()
                        dv_scale[t] = bump("dv")
                        em("dve", lambda h, o=scl_c, i=emc, j=zr: h.tensor_tensor(
                            o, i, j, ALU.mult).then_inc(s_dv, 1))

                for ti in range(4):
                    t = qb * 4 + ti
                    wait("pe", "dv", dv_pt)
                    if t > 0:
                        wait("pe", "dv", dv_res[t - 1])
                    v_pv = None
                    for jc in range(16):
                        lhs = PT[:, jc * 512 + ti * 128:jc * 512 + ti * 128 + 128]
                        for hf in range(2):
                            rhs = x_tok[:, jc * 1024 + hf * 512:jc * 1024 + (hf + 1) * 512]
                            pslc = PB[:, hf * 512:(hf + 1) * 512]
                            v_pv = mm(pslc, lhs, rhs, jc == 0, jc == 15,
                                      inc=(jc == 15 and hf == 1))
                    pe_pv_done = v_pv
                    wait("dve", "pe", pe_pv_done)
                    scl_c = e8[0:128, (t % 4):(t % 4) + 1]
                    if "fixscale" in CUTS:
                        scl_c = misc[0:128, 8:9]
                    stt = stream[:, t * 1024:(t + 1) * 1024]
                    v = bump("dv")
                    em("dve", lambda h, o=stt, sc=scl_c: h.scalar_tensor_tensor(
                        o, PB[:, 0:1024], sc, o, ALU.mult, ALU.add).then_inc(s_dv, 1))
                    if DUMP and s == 0 and t == 0:
                        wait("sp", "dv", v)
                        dma("sp", d_t0, stt, "st")
                        dma("sp", d_s0[:, 0:6], stats[0:128, 8:14], "st")
                        dma("sp", d_s0[:, 8:12], e8[0:128, 0:4], "st")
                    dv_res[t] = ln1(t)

            if DEBUG_STAGE == "attn":
                wait("sp", "dv", dv_res[TRUNC_QB * 4 - 1] if TRUNC_QB else 0)
                wait("sp", "sp", v_x)
                st_prev = dma("sp", out_r[s], stream_r, "st")
                if DUMP and s == 0:
                    dma("sp", d_xfm, xfm[:, :], "st")
                    dma("sp", d_lrt, regC[:, 0:8192], "st")
                    dma("sp", d_pt, PT[:, :], "st")
                    st_prev = dma("sp", d_st[:, 32:36], e8[:, 0:4], "st")
                wait("sp", "st", st_prev)
                pe_ffnA_done = cnt["pe"]
                pe_ffnB_done = cnt["pe"]
                dv_seq_done = cnt["dv"]
                continue

            dv_yn = 0
            for t in range(NT):
                dv_yn = rms(t, dv_res[t])
                stt2 = stream[:, t * 1024:(t + 1) * 1024]
                em("dve", lambda h, io=stt2: h.tensor_scalar_mul(io, io, 2.0))
            wait("sp", "dv", dv_yn)
            wait("sp", "sp", v_nf_prev)
            v_pk = dma("sp", tok_r, xtok_r, "sp")
            wait("sp", "sp", v_pk)
            wait("sp", "pe", pe_ltrt)
            v_nf = 0
            for dc in range(NDC):
                v_nf = dma("sp", xfm[:, dc * 2048:(dc + 1) * 2048],
                           tokscr[:, dc * 128:(dc + 1) * 128], "sp", transpose=True)
            v_nf_prev = v_nf
            wait("dve", "sp", v_nf)
            dv_wsc = 0
            for dc in range(NDC):
                slc = xfm[:, dc * 2048:(dc + 1) * 2048]
                last = (dc == NDC - 1)
                if last:
                    dv_wsc = bump("dv")
                em("dve", lambda h, io=slc, w=wcols[0:128, dc:dc + 1], lj=last:
                   h.tensor_scalar_mul(io, io, w).then_inc(s_dv, 1)
                   if lj else h.tensor_scalar_mul(io, io, w))

            wait("pe", "dv", dv_wsc)
            router_ac = [0, 0, 0, 0]
            for t in range(NT):
                pslc = PA[:, (t % 4) * 512:(t % 4) * 512 + 8]
                wait("pe", "ac", router_ac[t % 4])
                v = None
                for dc in range(NDC):
                    lhs = xfm[:, dc * 2048 + t * 128:dc * 2048 + t * 128 + 128]
                    rhs = rtrTb[:, dc * E:(dc + 1) * E]
                    v = mm(pslc, lhs, rhs, dc == 0, dc == NDC - 1, inc=(dc == NDC - 1))
                q = (t % 4) * E
                e8s = e8[:, q:q + E]
                a = (t % 2) * 16
                nm8 = stats[0:128, a + 8:a + 9]
                m1 = stats[0:128, a + 9:a + 10]
                m2 = stats[0:128, a + 10:a + 11]
                den = stats[0:128, a + 11:a + 12]
                wiv = stats[0:128, a + 12:a + 13]
                wa = stats[0:128, a + 13:a + 14]
                wb = stats[0:128, a + 15:a + 16]
                wait("dve", "pe", v)
                v = bump("dv")
                em("dve", lambda h, o=nm8, p=pslc: h.reduce_max(
                    o, p, axis=AX.X, negate=True).then_inc(s_dv, 1))
                wait("act", "dv", v)
                va = bump("ac")
                router_ac[t % 4] = va
                em("act", lambda h, o=e8s, p=pslc, b=nm8: nc.scalar.activation(
                    o, p, AF.Exp, bias=b, scale=1.0).then_inc(s_ac, 1))
                wait("dve", "ac", va)
                eq1 = sq[:, 0:8]
                tneg = sq[:, 8:16]
                pm = sq[:, 16:24]
                eq2 = sq[:, 24:32]
                c1 = sq[:, 32:40]
                em("dve", lambda h, o=m1, i=e8s: h.reduce_max(o, i, axis=AX.X))
                dvg()
                em("dve", lambda h, o=eq1, i=e8s, m=m1: h.tensor_single_scalar(o, i, m, ALU.is_equal))
                dvg()
                em("dve", lambda h, o=tneg, i=eq1: h.tensor_scalar(o, i, -1.0, 1.0, ALU.mult, ALU.add))
                dvg()
                em("dve", lambda h, o=pm, i=e8s, tn=tneg: h.tensor_tensor(o, i, tn, ALU.mult))
                dvg()
                em("dve", lambda h, o=m2, i=pm: h.reduce_max(o, i, axis=AX.X))
                dvg()
                em("dve", lambda h, o=den, i=m1, j=m2: h.tensor_tensor(o, i, j, ALU.add))
                dvg()
                em("dve", lambda h, o=wiv, i=den: h.reciprocal(o, i))
                dvg()
                em("dve", lambda h, o=wiv, i=wiv: h.tensor_tensor(o, i, mcol(ALC), ALU.mult))
                dvg()
                em("dve", lambda h, o=wa, i=m1, w=wiv: h.tensor_tensor(o, i, w, ALU.mult))
                em("dve", lambda h, o=wb, i=m2, w=wiv: h.tensor_tensor(o, i, w, ALU.mult))
                dvg()
                em("dve", lambda h, o=eq2, i=pm, m=m2: h.tensor_single_scalar(o, i, m, ALU.is_equal))
                em("dve", lambda h, o=c1, i=eq1, w=wa: h.tensor_single_scalar(o, i, w, ALU.mult))
                dvg()
                cslc = cab[:, t * E:(t + 1) * E]
                dv_rtr = bump("dv")
                em("dve", lambda h, o=cslc, i=eq2, w=wb, c=c1: h.scalar_tensor_tensor(
                    o, i, w, c, ALU.mult, ALU.add).then_inc(s_dv, 1))

            dv_stt_dn = 0
            pe_wd_free = 0
            dv_evac = [0, 0]
            pe_lastB = 0
            sched = [(e_, sw_, c_) for e_ in range(E) for sw_ in range(NSW) for c_ in range(NHB)]
            v_set = {}
            ptr = [0]

            def load_set(war_pe):
                if ptr[0] >= len(sched):
                    return
                e_, sw_, c_ = sched[ptr[0]]
                ptr[0] += 1
                buf = (sw_ * NHB + c_) % 2
                wait("sp", "pl", m_gu[e_])
                wait("sp", "pe", war_pe)
                gbuf = regD[:, buf * 2048:buf * 2048 + 1024]
                ubuf = regD[:, buf * 2048 + 1024:buf * 2048 + 2048]
                dma("sp", gbuf.rearrange("p (dc h) -> p dc h", h=128),
                    sguT_p[e_, :, :, c_ * 128:(c_ + 1) * 128], "sp")
                v_set[(e_, sw_, c_)] = dma("sp", ubuf.rearrange("p (dc h) -> p dc h", h=128),
                                           sguT_p[e_, :, :, HID + c_ * 128:HID + (c_ + 1) * 128], "sp")

            load_set(pe_ffnB_done)
            load_set(pe_ffnB_done)
            for e in range(E):
                wait("sp", "pl", m_dn[e])
                wait("sp", "pe", pe_wd_free)
                v_wd = dma("sp", wd_sb.rearrange("p (hc d) -> p hc d", d=D), sdnT_r[e], "sp")
                for sw in range(NSW):
                    for c in range(NHB):
                        buf = (sw * NHB + c) % 2
                        par = c % 2
                        pb = PA if par == 0 else PB
                        gp = pb[:, 0:1024]
                        up = pb[:, 1024:2048]
                        wait("pe", "sp", v_set[(e, sw, c)])
                        wait("pe", "dv", dv_evac[par])
                        if c < 2:
                            wait("pe", "dv", dv_stt_dn)
                            if e == 0 and sw == 0:
                                wait("pe", "dv", dv_rtr)
                                wait("pe", "ac", max(router_ac))
                        v_pe = None
                        for gu_i, pslc in ((0, gp), (1, up)):
                            base = buf * 2048 + gu_i * 1024
                            for dc in range(NDC):
                                lhs = regD[:, base + dc * 128:base + dc * 128 + 128]
                                for hf in range(2):
                                    rhs = xfm[:, dc * 2048 + sw * 1024 + hf * 512:
                                              dc * 2048 + sw * 1024 + (hf + 1) * 512]
                                    v_pe = mm(pslc[:, hf * 512:(hf + 1) * 512], lhs, rhs,
                                              dc == 0, dc == NDC - 1,
                                              inc=(gu_i == 1 and dc == NDC - 1 and hf == 1))
                        load_set(v_pe)
                        wait("act", "pe", v_pe)
                        wait("act", "dv", max(dv_evac))
                        em("act", lambda h, g=gp: nc.scalar.activation(
                            silu_sb[:, :], g[:, 0:512], AF.Silu, bias=mcol(ZERO), scale=1.0))
                        va = bump("ac")
                        em("act", lambda h, g=gp: nc.scalar.activation(
                            silu2_sb[:, :], g[:, 512:1024], AF.Silu, bias=mcol(ZERO), scale=1.0).then_inc(s_ac, 1))
                        wait("dve", "ac", va)
                        hslc = hidden[:, c * 1024:(c + 1) * 1024]
                        em("dve", lambda h, o=hslc, u=up: h.tensor_tensor(
                            o[:, 0:512], silu_sb[:, :], u[:, 0:512], ALU.mult))
                        dv_evac[par] = bump("dv")
                        em("dve", lambda h, o=hslc, u=up: h.tensor_tensor(
                            o[:, 512:1024], silu2_sb[:, :], u[:, 512:1024], ALU.mult).then_inc(s_dv, 1))
                    dv_hid_all = max(dv_evac)
                    wait("pe", "sp", v_wd)
                    for tt in range(NTT):
                        t = sw * NTT + tt
                        pb = PA if tt % 2 == 0 else PB
                        dp = pb[:, 0:1024]
                        wait("pe", "dv", dv_hid_all if tt == 0 else 0)
                        wait("pe", "dv", dv_stt_dn if tt >= 2 else 0)
                        v_pe = None
                        for hc in range(NHB):
                            lhs = hidden[:, hc * 1024 + tt * 128:hc * 1024 + tt * 128 + 128]
                            for hf in range(2):
                                rhs = wd_sb[:, hc * 1024 + hf * 512:hc * 1024 + (hf + 1) * 512]
                                v_pe = mm(dp[:, hf * 512:(hf + 1) * 512], lhs, rhs,
                                          hc == 0, hc == NHB - 1,
                                          inc=(hc == NHB - 1 and hf == 1))
                        pe_lastB = v_pe
                        wait("dve", "pe", v_pe)
                        stt = stream[:, t * 1024:(t + 1) * 1024]
                        ccol = cab[0:128, t * E + e:t * E + e + 1]
                        dv_stt_dn = bump("dv")
                        em("dve", lambda h, o=stt, d=dp, c=ccol: h.scalar_tensor_tensor(
                            o, d, c, o, ALU.mult, ALU.add).then_inc(s_dv, 1))
                    dv_evac = [dv_stt_dn, dv_stt_dn]
                pe_wd_free = pe_lastB
            pe_ffnA_done = cnt["pe"]
            pe_ffnB_done = cnt["pe"]

            last_dv = 0
            for t in range(NT):
                last_dv = ln1(t)
            dv_seq_done = last_dv
            wait("sp", "dv", last_dv)
            st_prev = dma("sp", out_r[s], stream_r, "st")

        wait("sp", "st", cnt["st"])

        with nc.Block() as block:
            @block.sync
            def _(h):
                for f in ops["sp"]:
                    f(h)

            @block.tensor
            def _(h):
                for f in ops["pe"]:
                    f(h)

            @block.vector
            def _(h):
                for f in ops["dve"]:
                    f(h)

            @block.scalar
            def _(h):
                for f in ops["act"]:
                    f(h)

            @block.gpsimd
            def _(h):
                for f in ops["pool"]:
                    f(h)

    return nc


def kernel(L0_attn_x, L0_attn_compare_left, L0_attn_compare_right,
           L0_attn_threshold, L0_attn_gate_bias, L0_ffn_norm_weight,
           L0_ffn_router_weight, L0_ffn_gate_up_weight, L0_ffn_down_weight,
           L0_ffn_residual_alpha):
    x = np.ascontiguousarray(np.asarray(L0_attn_x, dtype=np.float32))
    xT = np.ascontiguousarray(x.transpose(0, 2, 1).reshape(NCORES, SEQ, D, T))
    x = x.reshape(NCORES, SEQ, T, D)
    cl = np.ascontiguousarray(np.asarray(L0_attn_compare_left, dtype=np.float32))
    cr = np.ascontiguousarray(np.asarray(L0_attn_compare_right, dtype=np.float32))
    thr = np.asarray(L0_attn_threshold, dtype=np.float32).reshape(1, 1)
    gb = np.asarray(L0_attn_gate_bias, dtype=np.float32).reshape(1, 1)
    al = np.asarray(L0_ffn_residual_alpha, dtype=np.float32).reshape(1, 1)
    nw = np.ascontiguousarray(np.asarray(L0_ffn_norm_weight, dtype=np.float32).reshape(NDC, 128).T)
    rtrT = np.ascontiguousarray(np.asarray(L0_ffn_router_weight, dtype=np.float32).T)
    wguT = np.ascontiguousarray(np.asarray(L0_ffn_gate_up_weight, dtype=np.float32).transpose(0, 2, 1))
    wdnT = np.ascontiguousarray(np.asarray(L0_ffn_down_weight, dtype=np.float32).transpose(0, 2, 1))

    nc = build_program()
    in_maps = []
    for c in range(NCORES):
        in_maps.append(dict(
            x=x[c], xT=xT[c], cl=cl, cr=cr, thr=thr, gb=gb, alpha=al,
            nw=nw, rtrT=rtrT, wguT=wguT, wdnT=wdnT))
    res = run_bass_kernel_spmd(nc, in_maps, core_ids=list(range(NCORES)))
    return np.concatenate([r["out"] for r in res.results], axis=0)


if __name__ == "__main__":
    nc = build_program()
    print("instructions:", nc.n_instructions())


# version 11
# speedup vs baseline: 1.1020x; 1.1020x over previous
import numpy as np
from contextlib import ExitStack

import concourse.bass as bass
import concourse.mybir as mybir
from concourse.bass_utils import run_bass_kernel_spmd

F32 = mybir.dt.float32
BF16 = mybir.dt.bfloat16
AF = mybir.ActivationFunctionType
ALU = mybir.AluOpType
AX = mybir.AxisListType

NCORES = 8
SEQ = 4
T = 2048
NT = 16
D = 1024
NDC = 8
K = 256
KC = 2
HID = 2816
NHB = 22
E = 8
NSW = 2
NTT = 8
LN2C = 0.6931471805599453

DEBUG_STAGE = "full"
PREP_ON = True
TRUNC_QB = 4
CUTS = set()
DUMP = False


def build_program():
    nc = bass.Bass("TRN2", target_bir_lowering=False, debug=False,
                   dynamic_dma_scratch_size=16384,
                   detect_race_conditions=False)

    x_d = nc.dram_tensor("x", [SEQ, T, D], F32, kind="ExternalInput").ap()
    xT_d = nc.dram_tensor("xT", [SEQ, D, T], F32, kind="ExternalInput").ap()
    cl_d = nc.dram_tensor("cl", [D, K], F32, kind="ExternalInput").ap()
    cr_d = nc.dram_tensor("cr", [D, K], F32, kind="ExternalInput").ap()
    thr_d = nc.dram_tensor("thr", [1, 1], F32, kind="ExternalInput").ap()
    gb_d = nc.dram_tensor("gb", [1, 1], F32, kind="ExternalInput").ap()
    al_d = nc.dram_tensor("alpha", [1, 1], F32, kind="ExternalInput").ap()
    nw_d = nc.dram_tensor("nw", [128, NDC], F32, kind="ExternalInput").ap()
    nwr_d = nc.dram_tensor("nwr", [1, D], F32, kind="ExternalInput").ap()
    rtrT_d = nc.dram_tensor("rtrT", [D, E], F32, kind="ExternalInput").ap()
    wguT_d = nc.dram_tensor("wguT", [E, D, 2 * HID], F32, kind="ExternalInput").ap()
    wdnT_d = nc.dram_tensor("wdnT", [E, HID, D], F32, kind="ExternalInput").ap()
    out_d = nc.dram_tensor("out", [SEQ, T, D], F32, kind="ExternalOutput").ap()

    if DUMP:
        d_xfm = nc.dram_tensor("d_xfm", [128, NDC * 2048], BF16, kind="ExternalOutput").ap()
        d_lrt = nc.dram_tensor("d_lrt", [128, 8192], BF16, kind="ExternalOutput").ap()
        d_pt = nc.dram_tensor("d_pt", [128, 8192], BF16, kind="ExternalOutput").ap()
        d_st = nc.dram_tensor("d_st", [128, 64], F32, kind="ExternalOutput").ap()
        d_t0 = nc.dram_tensor("d_t0", [128, 1024], F32, kind="ExternalOutput").ap()
        d_s0 = nc.dram_tensor("d_s0", [128, 16], F32, kind="ExternalOutput").ap()
    sguT = nc.dram_tensor("sguT", [E, NDC, 128, 2 * HID], BF16, kind="Internal").ap()
    sdnT = nc.dram_tensor("sdnT", [E, HID, D], BF16, kind="Internal").ap()
    tokscr = nc.dram_tensor("tokscr", [T, D], BF16, kind="Internal").ap()

    ops = {e: [] for e in ("sp", "pe", "dve", "act", "pool")}

    def em(eng, f):
        ops[eng].append(f)

    cnt = {"sp": 0, "pl": 0, "pe": 0, "dv": 0, "ac": 0, "st": 0}

    with ExitStack() as ctx:
        def sbuf(name, shape, dt):
            return ctx.enter_context(nc.sbuf_tensor(name, shape, dt))

        guard = ctx.enter_context(nc.sbuf_tensor("topguard", [128, 4096], mybir.dt.uint8, side="right"))
        cab = sbuf("cab", [128, NT * E], F32)
        stats = sbuf("stats", [128, 32], F32)
        misc = sbuf("misc", [128, 16], F32)
        nwrow = sbuf("nwrow", [128, 1024], BF16)
        wcols = sbuf("wcols", [128, NDC], F32)
        rtrTf = sbuf("rtrTf", [128, NDC * E], F32)
        rtrTb = sbuf("rtrTb", [128, NDC * E], BF16)
        e8 = sbuf("e8", [128, 4 * E], F32)
        sc_sb = sbuf("sc_sb", [1, 4], F32)
        ones_sb = sbuf("ones_sb", [1, 128], F32)
        sq = sbuf("sq", [128, 64], F32)
        silu_sb = sbuf("silu_sb", [128, 512], BF16)
        silu2_sb = sbuf("silu2_sb", [128, 512], BF16)
        sgex = sbuf("sgex", [128, 1024], BF16)
        stgA = sbuf("stgA", [128, 1408], BF16)
        stream = sbuf("stream", [128, NT * 1024], F32)
        xfm = sbuf("xfm", [128, NDC * 2048], BF16)
        regB = sbuf("regB", [128, NHB * 1024], BF16)
        regC = sbuf("regC", [128, NHB * 1024], BF16)
        regD = sbuf("regD", [128, 4096], BF16)

        PA = ctx.enter_context(nc.psum_tensor("PA", [128, 2048], F32))
        PB = ctx.enter_context(nc.psum_tensor("PB", [128, 2048], F32))

        s_sp = ctx.enter_context(nc.semaphore("s_sp"))
        s_pl = ctx.enter_context(nc.semaphore("s_pl"))
        s_pe = ctx.enter_context(nc.semaphore("s_pe"))
        s_dv = ctx.enter_context(nc.semaphore("s_dv"))
        s_ac = ctx.enter_context(nc.semaphore("s_ac"))
        s_st = ctx.enter_context(nc.semaphore("s_st"))
        SEM = {"sp": s_sp, "pl": s_pl, "pe": s_pe, "dv": s_dv, "ac": s_ac, "st": s_st}

        def dma(eng, out, in_, sem, transpose=False):
            cnt[sem] += 16
            s = SEM[sem]
            em(eng, lambda h, o=out, i=in_, s=s, tr=transpose:
               h.dma_start(out=o, in_=i, transpose=tr).then_inc(s, 16))
            return cnt[sem]

        def wait(eng, sem, v):
            if v <= 0:
                return
            s = SEM[sem]
            em(eng, lambda h, s=s, v=v: h.wait_ge(s, v))

        def bump(sem, n=1):
            cnt[sem] += n
            return cnt[sem]

        def dvg():
            em("dve", lambda h: h.tensor_scalar_mul(misc[0:128, 9:10], misc[0:128, 5:6], 0.0))

        def ac_settle_inc():
            em("act", lambda h: nc.scalar.copy(misc[0:128, 10:11], misc[0:128, 5:6]))
            v = bump("ac")
            em("act", lambda h: nc.scalar.copy(
                misc[0:128, 10:11], misc[0:128, 5:6]).then_inc(s_ac, 1))
            return v

        def dv_settle_inc():
            dvg()
            dvg()
            dvg()
            v = bump("dv")
            em("dve", lambda h: h.tensor_scalar_mul(
                misc[0:128, 9:10], misc[0:128, 5:6], 0.0).then_inc(s_dv, 1))
            return v

        def mm(pslc, lhs, rhs, start, stop, inc=False):
            if inc:
                v = bump("pe")
                em("pe", lambda h, p=pslc, l=lhs, r=rhs, a=start, b=stop:
                   nc.tensor.matmul(p, l, r, start=a, stop=b).then_inc(s_pe, 1))
                return v
            em("pe", lambda h, p=pslc, l=lhs, r=rhs, a=start, b=stop:
               nc.tensor.matmul(p, l, r, start=a, stop=b))
            return None

        THR, GB, ALC, NTHR, GLN2, ZERO, EPS5, EPS6 = range(8)

        def mcol(i):
            return misc[0:128, i:i + 1]

        ones_d = nc.inline_tensor(np.ones((1, 128), dtype=np.float32), "onesrow")
        prep_state = {"last": 0, "war_pe": 0}
        dma("sp", wcols[:, :], nw_d, "sp")
        dma("sp", sc_sb[0:1, 0:1], thr_d, "sp")
        dma("sp", sc_sb[0:1, 1:2], gb_d, "sp")
        dma("sp", sc_sb[0:1, 2:3], al_d, "sp")

        dma("sp", rtrTf[:, :].rearrange("p (dc e) -> p dc e", e=E),
            rtrT_d.rearrange("(dc p) e -> p dc e", p=128), "sp")
        v_pro = dma("sp", ones_sb[0:1, :], ones_d.ap(), "sp")
        v_nwr = dma("pool", stgA[0:1, 0:1024], nwr_d, "pl")
        if "pro" in CUTS:
            prologue_dv = 0
        else:
            wait("pe", "sp", v_pro)
        if "pro" not in CUTS:
            v = mm(PA[:, 0:3], ones_sb[0:1, :], sc_sb[0:1, 0:3], True, True, inc=True)
            wait("dve", "sp", v_pro)
            wait("dve", "pe", v)
            em("dve", lambda h: h.tensor_copy(misc[:, 0:3], PA[:, 0:3]))
            em("dve", lambda h: h.memset(mcol(ZERO), 0.0))
            em("dve", lambda h: h.tensor_scalar(mcol(ALC), mcol(ALC), 2.0, -0.5, ALU.min, ALU.max))
            em("dve", lambda h: h.tensor_scalar_mul(mcol(NTHR), mcol(THR), -1.0))
            em("dve", lambda h: h.tensor_scalar_add(mcol(GLN2), mcol(GB), -LN2C))
            em("dve", lambda h: h.memset(mcol(EPS5), 1e-5))
            em("dve", lambda h: h.memset(misc[0:128, 8:9], 1.0))
            em("dve", lambda h: h.memset(mcol(EPS6), 1e-6))
            em("dve", lambda h: h.memset(stgA[0:1, 1024:1152], 1.0))
            wait("pe", "dv", dv_settle_inc())
            wait("pe", "pl", v_nwr)
            mm(PB[:, 0:512], stgA[0:1, 1024:1152], stgA[0:1, 0:512], True, True)
            v = mm(PB[:, 512:1024], stgA[0:1, 1024:1152], stgA[0:1, 512:1024], True, True, inc=True)
            wait("dve", "pe", v)
            em("dve", lambda h: h.tensor_copy(nwrow[:, :], PB[:, 0:1024]))
            prep_state["war_pe"] = v
            prologue_dv = bump("dv")
            em("dve", lambda h: h.tensor_copy(rtrTb[:, :], rtrTf[:, :]).then_inc(s_dv, 1))
            wait("act", "dv", prologue_dv)

        m_gu = [0] * E
        m_dn = [0] * E
        wguT_r = wguT_d.rearrange("e (dc p) h -> e dc p h", p=128)
        wdnT_r2 = wdnT_d.rearrange("e (c p) d -> e p c d", p=128)
        sdnT_r2 = sdnT.rearrange("e (c p) d -> e p c d", p=128)
        def emit_prep():
            last_store = prep_state["last"]
            wait("pool", "pe", prep_state["war_pe"])
            for e in range(E if PREP_ON else 0):
                for dc in range(NDC):
                    for hf in range(2):
                        for c0 in (0, 1408):
                            wait("pool", "pl", last_store)
                            vc = dma("pool", stgA[:, :],
                                     wguT_r[e, dc, :, hf * HID + c0:hf * HID + c0 + 1408], "pl")
                            wait("pool", "pl", vc)
                            last_store = dma("pool", sguT[e, dc, :, hf * HID + c0:hf * HID + c0 + 1408],
                                             stgA[:, :], "pl")
                m_gu[e] = last_store
                for c2 in range(NHB):
                    wait("pool", "pl", last_store)
                    vc = dma("pool", stgA[:, 0:1024], wdnT_r2[e, :, c2, :], "pl")
                    wait("pool", "pl", vc)
                    last_store = dma("pool", sdnT_r2[e, :, c2, :], stgA[:, 0:1024], "pl")
                m_dn[e] = last_store
            prep_state["last"] = last_store

        x_tok = regB[:, 0:NT * 1024]
        e_sb = regB[:, NT * 1024:NT * 1024 + 2048]
        hidden = regB[:, 0:NHB * 1024]
        LT = regC[:, 0:4096]
        RT = regC[:, 4096:8192]
        CLb = regC[:, 8192:10240]
        CRb = regC[:, 10240:12288]
        PT = regC[:, 12288:20480]
        wd_sb = regC[:, 0:NHB * 1024]

        out_r = out_d.rearrange("s (tc p) d -> s p tc d", p=128)
        x_r = x_d.rearrange("s (tc p) d -> s p tc d", p=128)
        out_rt = out_d.rearrange("s (tc p) d -> s tc p d", p=128)
        x_rt = x_d.rearrange("s (tc p) d -> s tc p d", p=128)
        tok_r = tokscr.rearrange("(tc p) d -> p tc d", p=128)
        stream_r = stream[:, :].rearrange("p (tc d) -> p tc d", d=1024)
        xtok_r = x_tok.rearrange("p (tc d) -> p tc d", d=1024)
        sdnT_r = sdnT.rearrange("e (hc p) d -> e p hc d", p=128)
        sguT_p = sguT.rearrange("e dc p h -> e p dc h")

        pe_ffnA_done = 0
        pe_ffnB_done = 0
        dv_seq_done = prologue_dv
        st_prev = 0
        v_nf_prev = 0

        def ln1(t):
            stt = stream[:, t * 1024:(t + 1) * 1024]
            a = (t % 2) * 16
            ns_c = stats[0:128, a + 0:a + 1]
            nm_c = stats[0:128, a + 1:a + 2]
            vs_c = stats[0:128, a + 2:a + 3]
            sd_c = stats[0:128, a + 3:a + 4]
            rs_c = stats[0:128, a + 4:a + 5]
            vs2_c = stats[0:128, a + 14:a + 15]
            em("dve", lambda h, i=stt, o=ns_c: h.reduce_sum(o, i, axis=AX.X, negate=True))
            dvg()
            v = bump("dv")
            em("dve", lambda h, i=ns_c, o=nm_c: h.tensor_scalar_mul(o, i, 1.0 / D).then_inc(s_dv, 1))
            wait("act", "dv", v)
            em("act", lambda h, i=stt, b=nm_c, ao=vs_c: nc.scalar.activation(
                silu_sb[:, :], i[:, 0:512], AF.Square, bias=b, scale=1.0, accum_out=ao))
            va = bump("ac")
            em("act", lambda h, i=stt, b=nm_c, ao=vs2_c: nc.scalar.activation(
                silu_sb[:, :], i[:, 512:1024], AF.Square, bias=b, scale=1.0, accum_out=ao).then_inc(s_ac, 1))
            wait("dve", "ac", va)
            v = bump("dv")
            em("dve", lambda h, o=vs_c, i=vs_c, j=vs2_c: h.tensor_tensor(o, i, j, ALU.add).then_inc(s_dv, 1))
            dvg()
            wait("act", "dv", v)
            va = bump("ac")
            em("act", lambda h, i=vs_c, o=sd_c: nc.scalar.activation(
                o, i, AF.Sqrt, bias=mcol(EPS5), scale=1.0 / D).then_inc(s_ac, 1))
            wait("dve", "ac", va)
            em("dve", lambda h, i=sd_c, o=rs_c: h.reciprocal(o, i))
            dvg()
            v = bump("dv")
            em("dve", lambda h, io=stt, b=nm_c, r=rs_c: h.tensor_scalar(
                io, io, b, r, ALU.add, ALU.mult).then_inc(s_dv, 1))
            return v

        def rms(t, dv_ready):
            stt = stream[:, t * 1024:(t + 1) * 1024]
            wait("act", "dv", dv_ready)
            a = (t % 2) * 16
            ss_c = stats[0:128, a + 5:a + 6]
            s2_c = stats[0:128, a + 6:a + 7]
            r2_c = stats[0:128, a + 7:a + 8]
            ss2_c = stats[0:128, a + 15:a + 16]
            em("act", lambda h, i=stt, ao=ss_c: nc.scalar.activation(
                silu_sb[:, :], i[:, 0:512], AF.Square, bias=mcol(ZERO), scale=1.0, accum_out=ao))
            va = bump("ac")
            em("act", lambda h, i=stt, ao=ss2_c: nc.scalar.activation(
                silu_sb[:, :], i[:, 512:1024], AF.Square, bias=mcol(ZERO), scale=1.0, accum_out=ao).then_inc(s_ac, 1))
            wait("dve", "ac", va)
            v = bump("dv")
            em("dve", lambda h, o=ss_c, i=ss_c, j=ss2_c: h.tensor_tensor(o, i, j, ALU.add).then_inc(s_dv, 1))
            dvg()
            wait("act", "dv", v)
            va = bump("ac")
            em("act", lambda h, i=ss_c, o=s2_c: nc.scalar.activation(
                o, i, AF.Sqrt, bias=mcol(EPS6), scale=1.0 / D).then_inc(s_ac, 1))
            wait("dve", "ac", va)
            em("dve", lambda h, i=s2_c, o=r2_c: h.reciprocal(o, i))
            dvg()
            yn = x_tok[:, t * 1024:(t + 1) * 1024]
            v = bump("dv")
            em("dve", lambda h, i=stt, r=r2_c, o=yn: h.tensor_scalar_mul(
                o, i, r).then_inc(s_dv, 1))
            return v

        v_xt = [0] * NT
        for t in range(NT):
            v_xt[t] = dma("sp", stream[:, t * 1024:(t + 1) * 1024], x_rt[0, t], "sp")

        for s in range(SEQ):
            v_ct = 0
            v_fm = 0
            if "casts" not in CUTS:
                if "c_xfm" not in CUTS:
                    wait("pool", "pe", pe_ffnA_done)
                    for dc in range(NDC):
                        v_fm = dma("pool", xfm[:, dc * 2048:(dc + 1) * 2048], xT_d[s, dc * 128:(dc + 1) * 128, :], "pl")
                if "c_clcr" not in CUTS:
                    wait("pool", "pe", pe_ffnB_done)
                    dma("pool", CLb.rearrange("p (dc k) -> p dc k", k=K),
                        cl_d.rearrange("(dc p) k -> p dc k", p=128), "pl")
                    v_fm = dma("pool", CRb.rearrange("p (dc k) -> p dc k", k=K),
                               cr_d.rearrange("(dc p) k -> p dc k", p=128), "pl")
                if "c_xtok" not in CUTS:
                    wait("pool", "pl", v_fm)
                    wait("pool", "pe", pe_ffnB_done)
                    v_ct = dma("pool", xtok_r, x_r[s], "pl")
            if s == 0:
                emit_prep()

            if "ltrt" in CUTS:
                attn_evacs_ac = 0
                attn_evacs_dv = 0
                pe_ltrt = 0
            wait("pe", "pl", v_fm)
            wait("pe", "dv", dv_seq_done)
            ac_ev = 0
            for cb, dst, scl, pbank in (() if "ltrt" in CUTS else ((CLb, LT, 1.0 / 32.0, PA), (CRb, RT, 1.0, PB))):
                for kc in range(KC):
                    wait("pe", "dv", ac_ev)
                    vp = None
                    for jg in range(4):
                        pslc = pbank[:, jg * 512:(jg + 1) * 512]
                        for dc in range(NDC):
                            lhs = cb[:, dc * 256 + kc * 128:dc * 256 + kc * 128 + 128]
                            rhs = xfm[:, dc * 2048 + jg * 512:dc * 2048 + (jg + 1) * 512]
                            vp = mm(pslc, lhs, rhs, dc == 0, dc == NDC - 1,
                                    inc=(jg == 3 and dc == NDC - 1))
                    wait("dve", "pe", vp)
                    for jg in range(4):
                        pslc = pbank[:, jg * 512:(jg + 1) * 512]
                        dslc = dst[:, kc * 2048 + jg * 512:kc * 2048 + (jg + 1) * 512]
                        em("dve", lambda h, d=dslc, p=pslc, sc=scl: h.tensor_scalar_mul(d, p, sc))
                    ac_ev = dv_settle_inc()
            if "ltrt" not in CUTS:
                attn_evacs_ac = None
                attn_evacs_dv = ac_ev
                pe_ltrt = cnt["pe"]

            pe_pv_done = 0
            dv_res = [0] * NT
            ac_slot = [0, 0]
            dv_sgex = 0
            ac_prev_stats = 0
            dv_evac_ltrt = attn_evacs_dv
            dv_scale = [0] * NT
            for qb in range(TRUNC_QB):
                dv_pt = 0
                for jc in range(16):
                    pslc = PB[:, 1024 + (jc % 2) * 512:1536 + (jc % 2) * 512]
                    wait("pe", "dv", dv_evac_ltrt)
                    wait("pe", "ac", ac_slot[jc % 2])
                    v_st = None
                    for kc in range(KC):
                        lhs = RT[:, kc * 2048 + jc * 128:kc * 2048 + jc * 128 + 128]
                        rhs = LT[:, kc * 2048 + qb * 512:kc * 2048 + (qb + 1) * 512]
                        v_st = mm(pslc, lhs, rhs, kc == 0, kc == KC - 1, inc=(kc == KC - 1))
                    wait("act", "pe", v_st)
                    wait("act", "dv", dv_sgex)
                    sg = sgex[:, 0:512]
                    ex = sgex[:, 512:1024]
                    em("act", lambda h, o=sg, p=pslc: nc.scalar.sign(o, p, bias=mcol(NTHR)))
                    em("act", lambda h, o=ex, p=pslc: nc.scalar.activation(
                        o, p, AF.Exp, bias=mcol(GLN2), scale=1.0))
                    ac_slot[jc % 2] = ac_settle_inc()
                    wait("dve", "ac", ac_slot[jc % 2])
                    wait("dve", "pe", pe_pv_done)
                    ptp = PT[:, jc * 512:(jc + 1) * 512]
                    dv_sgex = bump("dv")
                    dv_pt = dv_sgex
                    em("dve", lambda h, o=ptp, sg=sg, ex=ex: h.scalar_tensor_tensor(
                        o, sg, 1.0, ex, ALU.add, ALU.mult).then_inc(s_dv, 1))

                    if jc % 4 == 3:
                        t = qb * 4 + jc // 4
                        wait("pe", "ac", ac_prev_stats)
                        v_sc = None
                        for jg in range(4):
                            pslc2 = PA[:, jg * 512:(jg + 1) * 512]
                            for kc in range(KC):
                                lhs = LT[:, kc * 2048 + t * 128:kc * 2048 + t * 128 + 128]
                                rhs = RT[:, kc * 2048 + jg * 512:kc * 2048 + (jg + 1) * 512]
                                v_sc = mm(pslc2, lhs, rhs, kc == 0, kc == KC - 1,
                                          inc=(jg == 3 and kc == KC - 1))
                        a = (t % 2) * 16
                        Zc = stats[0:128, a + 11:a + 12]
                        zr = stats[0:128, a + 13:a + 14]
                        scl_c = e8[0:128, (t % 4):(t % 4) + 1]
                        wait("act", "pe", v_sc)
                        for bk in range(4):
                            em("act", lambda h, bk=bk: nc.scalar.activation(
                                e_sb[:, bk * 512:(bk + 1) * 512],
                                PA[:, bk * 512:(bk + 1) * 512], AF.Exp, bias=mcol(GB),
                                scale=1.0))
                        ac_prev_stats = ac_settle_inc()
                        wait("dve", "ac", ac_prev_stats)
                        em("dve", lambda h, o=Zc: h.reduce_sum(o, e_sb, axis=AX.X))
                        dvg()
                        em("dve", lambda h, o=zr, i=Zc: h.reciprocal(o, i))
                        dvg()
                        dv_scale[t] = bump("dv")
                        em("dve", lambda h, o=scl_c, i=zr: h.tensor_scalar_mul(
                            o, i, 1.0).then_inc(s_dv, 1))

                for ti in range(4):
                    t = qb * 4 + ti
                    wait("pe", "dv", dv_pt)
                    wait("pe", "pl", v_ct)
                    if t > 0:
                        wait("pe", "dv", dv_res[t - 1])
                    v_pv = None
                    for jc in range(16):
                        lhs = PT[:, jc * 512 + ti * 128:jc * 512 + ti * 128 + 128]
                        for hf in range(2):
                            rhs = x_tok[:, jc * 1024 + hf * 512:jc * 1024 + (hf + 1) * 512]
                            pslc = PB[:, hf * 512:(hf + 1) * 512]
                            v_pv = mm(pslc, lhs, rhs, jc == 0, jc == 15,
                                      inc=(jc == 15 and hf == 1))
                    pe_pv_done = v_pv
                    wait("dve", "pe", pe_pv_done)
                    wait("dve", "sp", v_xt[t])
                    scl_c = e8[0:128, (t % 4):(t % 4) + 1]
                    if "fixscale" in CUTS:
                        scl_c = misc[0:128, 8:9]
                    stt = stream[:, t * 1024:(t + 1) * 1024]
                    v = bump("dv")
                    em("dve", lambda h, o=stt, sc=scl_c: h.scalar_tensor_tensor(
                        o, PB[:, 0:1024], sc, o, ALU.mult, ALU.add).then_inc(s_dv, 1))
                    if DUMP and s == 0 and t == 0:
                        wait("sp", "dv", v)
                        dma("sp", d_t0, stt, "st")
                        dma("sp", d_s0[:, 0:6], stats[0:128, 8:14], "st")
                        dma("sp", d_s0[:, 8:12], e8[0:128, 0:4], "st")
                    dv_res[t] = ln1(t)

            if DEBUG_STAGE == "attn":
                wait("sp", "dv", dv_res[TRUNC_QB * 4 - 1] if TRUNC_QB else 0)
                wait("sp", "sp", v_xt[NT - 1])
                st_prev = dma("sp", out_r[s], stream_r, "st")
                if DUMP and s == 0:
                    dma("sp", d_xfm, xfm[:, :], "st")
                    dma("sp", d_lrt, regC[:, 0:8192], "st")
                    dma("sp", d_pt, PT[:, :], "st")
                    st_prev = dma("sp", d_st[:, 32:36], e8[:, 0:4], "st")
                wait("sp", "st", st_prev)
                if s + 1 < SEQ:
                    for t in range(NT):
                        v_xt[t] = dma("sp", stream[:, t * 1024:(t + 1) * 1024],
                                      x_rt[s + 1, t], "sp")
                pe_ffnA_done = cnt["pe"]
                pe_ffnB_done = cnt["pe"]
                dv_seq_done = cnt["dv"]
                continue

            wait("dve", "sp", v_nf_prev)
            st_tok = 0
            dv_yn_t = [0] * NT
            for t in range(NT):
                dv_yn_t[t] = rms(t, dv_res[t])
                stt2 = stream[:, t * 1024:(t + 1) * 1024]
                em("dve", lambda h, io=stt2: h.tensor_scalar_mul(io, io, 2.0))
                if t >= 1:
                    wait("act", "dv", dv_yn_t[t - 1])
                    st_tok = dma("act", tok_r[:, t - 1, :], xtok_r[:, t - 1, :], "sp")
            wait("act", "dv", dv_settle_inc())
            st_tok = dma("act", tok_r[:, NT - 1, :], xtok_r[:, NT - 1, :], "sp")
            for q in ("sp", "act"):
                wait(q, "sp", st_tok)
                wait(q, "pe", pe_ltrt)
            v_nf = 0
            qmap = ("sp", "sp", "sp", "sp", "sp", "sp", "sp", "sp")
            for dc in range(NDC):
                v_nf = dma(qmap[dc], xfm[:, dc * 2048:(dc + 1) * 2048],
                           tokscr[:, dc * 128:(dc + 1) * 128], "sp", transpose=True)
            v_nf_prev = v_nf
            wait("dve", "sp", v_nf)
            for dc in range(NDC):
                slc = xfm[:, dc * 2048:(dc + 1) * 2048]
                em("dve", lambda h, io=slc, w=wcols[0:128, dc:dc + 1]:
                   h.tensor_scalar_mul(io, io, w))
            dv_wsc = dv_settle_inc()

            wait("pe", "dv", dv_wsc)
            router_ac = [0, 0, 0, 0]
            for t in range(NT):
                pslc = PA[:, (t % 4) * 512:(t % 4) * 512 + 8]
                wait("pe", "ac", router_ac[t % 4])
                v = None
                for dc in range(NDC):
                    lhs = xfm[:, dc * 2048 + t * 128:dc * 2048 + t * 128 + 128]
                    rhs = rtrTb[:, dc * E:(dc + 1) * E]
                    v = mm(pslc, lhs, rhs, dc == 0, dc == NDC - 1, inc=(dc == NDC - 1))
                q = (t % 4) * E
                e8s = e8[:, q:q + E]
                a = (t % 2) * 16
                nm8 = stats[0:128, a + 8:a + 9]
                m1 = stats[0:128, a + 9:a + 10]
                m2 = stats[0:128, a + 10:a + 11]
                den = stats[0:128, a + 11:a + 12]
                wiv = stats[0:128, a + 12:a + 13]
                wa = stats[0:128, a + 13:a + 14]
                wb = stats[0:128, a + 15:a + 16]
                wait("dve", "pe", v)
                v = bump("dv")
                em("dve", lambda h, o=nm8, p=pslc: h.reduce_max(
                    o, p, axis=AX.X, negate=True).then_inc(s_dv, 1))
                wait("act", "dv", v)
                va = bump("ac")
                router_ac[t % 4] = va
                em("act", lambda h, o=e8s, p=pslc, b=nm8: nc.scalar.activation(
                    o, p, AF.Exp, bias=b, scale=1.0).then_inc(s_ac, 1))
                wait("dve", "ac", va)
                eq1 = sq[:, 0:8]
                tneg = sq[:, 8:16]
                pm = sq[:, 16:24]
                eq2 = sq[:, 24:32]
                c1 = sq[:, 32:40]
                em("dve", lambda h, o=m1, i=e8s: h.reduce_max(o, i, axis=AX.X))
                dvg()
                em("dve", lambda h, o=eq1, i=e8s, m=m1: h.tensor_single_scalar(o, i, m, ALU.is_equal))
                dvg()
                em("dve", lambda h, o=tneg, i=eq1: h.tensor_scalar(o, i, -1.0, 1.0, ALU.mult, ALU.add))
                dvg()
                em("dve", lambda h, o=pm, i=e8s, tn=tneg: h.tensor_tensor(o, i, tn, ALU.mult))
                dvg()
                em("dve", lambda h, o=m2, i=pm: h.reduce_max(o, i, axis=AX.X))
                dvg()
                em("dve", lambda h, o=den, i=m1, j=m2: h.tensor_tensor(o, i, j, ALU.add))
                dvg()
                em("dve", lambda h, o=wiv, i=den: h.reciprocal(o, i))
                dvg()
                em("dve", lambda h, o=wiv, i=wiv: h.tensor_tensor(o, i, mcol(ALC), ALU.mult))
                dvg()
                em("dve", lambda h, o=wa, i=m1, w=wiv: h.tensor_tensor(o, i, w, ALU.mult))
                em("dve", lambda h, o=wb, i=m2, w=wiv: h.tensor_tensor(o, i, w, ALU.mult))
                dvg()
                em("dve", lambda h, o=eq2, i=pm, m=m2: h.tensor_single_scalar(o, i, m, ALU.is_equal))
                em("dve", lambda h, o=c1, i=eq1, w=wa: h.tensor_single_scalar(o, i, w, ALU.mult))
                dvg()
                cslc = cab[:, t * E:(t + 1) * E]
                dv_rtr = bump("dv")
                em("dve", lambda h, o=cslc, i=eq2, w=wb, c=c1: h.scalar_tensor_tensor(
                    o, i, w, c, ALU.mult, ALU.add).then_inc(s_dv, 1))

            dv_stt_dn = 0
            pe_wd_free = 0
            dv_evac = [0, 0]
            pe_lastB = 0
            sched = [(e_, sw_, c_) for e_ in range(E) for sw_ in range(NSW) for c_ in range(NHB)]
            v_set = {}
            ptr = [0]

            def load_set(war_pe):
                if ptr[0] >= len(sched):
                    return
                e_, sw_, c_ = sched[ptr[0]]
                ptr[0] += 1
                buf = (sw_ * NHB + c_) % 2
                wait("sp", "pl", m_gu[e_])
                wait("sp", "pe", war_pe)
                gbuf = regD[:, buf * 2048:buf * 2048 + 1024]
                ubuf = regD[:, buf * 2048 + 1024:buf * 2048 + 2048]
                dma("sp", gbuf.rearrange("p (dc h) -> p dc h", h=128),
                    sguT_p[e_, :, :, c_ * 128:(c_ + 1) * 128], "sp")
                v_set[(e_, sw_, c_)] = dma("sp", ubuf.rearrange("p (dc h) -> p dc h", h=128),
                                           sguT_p[e_, :, :, HID + c_ * 128:HID + (c_ + 1) * 128], "sp")

            load_set(pe_ffnB_done)
            load_set(pe_ffnB_done)
            for e in range(E):
                wait("sp", "pl", m_dn[e])
                wait("sp", "pe", pe_wd_free)
                v_wd = dma("sp", wd_sb.rearrange("p (hc d) -> p hc d", d=D), sdnT_r[e], "sp")
                for sw in range(NSW):
                    for c in range(NHB):
                        buf = (sw * NHB + c) % 2
                        par = c % 2
                        pb = PA if par == 0 else PB
                        gp = pb[:, 0:1024]
                        up = pb[:, 1024:2048]
                        wait("pe", "sp", v_set[(e, sw, c)])
                        wait("pe", "dv", dv_evac[par])
                        if c < 2:
                            wait("pe", "dv", dv_stt_dn)
                            if e == 0 and sw == 0:
                                wait("pe", "dv", dv_rtr)
                                wait("pe", "ac", max(router_ac))
                        v_pe = None
                        for gu_i, pslc in ((0, gp), (1, up)):
                            base = buf * 2048 + gu_i * 1024
                            for dc in range(NDC):
                                lhs = regD[:, base + dc * 128:base + dc * 128 + 128]
                                for hf in range(2):
                                    rhs = xfm[:, dc * 2048 + sw * 1024 + hf * 512:
                                              dc * 2048 + sw * 1024 + (hf + 1) * 512]
                                    v_pe = mm(pslc[:, hf * 512:(hf + 1) * 512], lhs, rhs,
                                              dc == 0, dc == NDC - 1,
                                              inc=(gu_i == 1 and dc == NDC - 1 and hf == 1))
                        load_set(v_pe)
                        wait("act", "pe", v_pe)
                        wait("act", "dv", max(dv_evac))
                        em("act", lambda h, g=gp: nc.scalar.activation(
                            silu_sb[:, :], g[:, 0:512], AF.Silu, bias=mcol(ZERO), scale=1.0))
                        em("act", lambda h, g=gp: nc.scalar.activation(
                            silu2_sb[:, :], g[:, 512:1024], AF.Silu, bias=mcol(ZERO), scale=1.0))
                        va = ac_settle_inc()
                        wait("dve", "ac", va)
                        hslc = hidden[:, c * 1024:(c + 1) * 1024]
                        em("dve", lambda h, o=hslc, u=up: h.tensor_tensor(
                            o[:, 0:512], silu_sb[:, :], u[:, 0:512], ALU.mult))
                        dv_evac[par] = bump("dv")
                        em("dve", lambda h, o=hslc, u=up: h.tensor_tensor(
                            o[:, 512:1024], silu2_sb[:, :], u[:, 512:1024], ALU.mult).then_inc(s_dv, 1))
                    dv_hid_all = max(dv_evac)
                    wait("pe", "sp", v_wd)
                    for tt in range(NTT):
                        t = sw * NTT + tt
                        pb = PA if tt % 2 == 0 else PB
                        dp = pb[:, 0:1024]
                        wait("pe", "dv", dv_hid_all if tt == 0 else 0)
                        wait("pe", "dv", dv_stt_dn if tt >= 2 else 0)
                        v_pe = None
                        for hc in range(NHB):
                            lhs = hidden[:, hc * 1024 + tt * 128:hc * 1024 + tt * 128 + 128]
                            for hf in range(2):
                                rhs = wd_sb[:, hc * 1024 + hf * 512:hc * 1024 + (hf + 1) * 512]
                                v_pe = mm(dp[:, hf * 512:(hf + 1) * 512], lhs, rhs,
                                          hc == 0, hc == NHB - 1,
                                          inc=(hc == NHB - 1 and hf == 1))
                        pe_lastB = v_pe
                        wait("dve", "pe", v_pe)
                        stt = stream[:, t * 1024:(t + 1) * 1024]
                        ccol = cab[0:128, t * E + e:t * E + e + 1]
                        dv_stt_dn = bump("dv")
                        em("dve", lambda h, o=stt, d=dp, c=ccol: h.scalar_tensor_tensor(
                            o, d, c, o, ALU.mult, ALU.add).then_inc(s_dv, 1))
                    dv_evac = [dv_stt_dn, dv_stt_dn]
                pe_wd_free = pe_lastB
            pe_ffnA_done = cnt["pe"]
            pe_ffnB_done = cnt["pe"]

            last_dv = 0
            nxt = [0] * NT
            dv_ln = [0] * NT

            def emit_store(t):
                st_t = dma("act", out_rt[s, t], stream[:, t * 1024:(t + 1) * 1024], "st")
                if s + 1 < SEQ:
                    wait("sp", "st", st_t)
                    nxt[t] = dma("sp", stream[:, t * 1024:(t + 1) * 1024],
                                 x_rt[s + 1, t], "sp")

            for t in range(NT):
                dv_ln[t] = ln1(t)
                last_dv = dv_ln[t]
                if t >= 1:
                    wait("act", "dv", dv_ln[t - 1])
                    emit_store(t - 1)
            wait("act", "dv", dv_settle_inc())
            emit_store(NT - 1)
            v_xt = nxt
            dv_seq_done = last_dv

        wait("sp", "st", cnt["st"])

        with nc.Block() as block:
            @block.sync
            def _(h):
                for f in ops["sp"]:
                    f(h)

            @block.tensor
            def _(h):
                for f in ops["pe"]:
                    f(h)

            @block.vector
            def _(h):
                for f in ops["dve"]:
                    f(h)

            @block.scalar
            def _(h):
                for f in ops["act"]:
                    f(h)

            @block.gpsimd
            def _(h):
                for f in ops["pool"]:
                    f(h)

    return nc


def kernel(L0_attn_x, L0_attn_compare_left, L0_attn_compare_right,
           L0_attn_threshold, L0_attn_gate_bias, L0_ffn_norm_weight,
           L0_ffn_router_weight, L0_ffn_gate_up_weight, L0_ffn_down_weight,
           L0_ffn_residual_alpha):
    x = np.ascontiguousarray(np.asarray(L0_attn_x, dtype=np.float32))
    xT = np.ascontiguousarray(x.transpose(0, 2, 1).reshape(NCORES, SEQ, D, T))
    x = x.reshape(NCORES, SEQ, T, D)
    cl = np.ascontiguousarray(np.asarray(L0_attn_compare_left, dtype=np.float32))
    cr = np.ascontiguousarray(np.asarray(L0_attn_compare_right, dtype=np.float32))
    thr = np.asarray(L0_attn_threshold, dtype=np.float32).reshape(1, 1)
    gb = np.asarray(L0_attn_gate_bias, dtype=np.float32).reshape(1, 1)
    al = np.asarray(L0_ffn_residual_alpha, dtype=np.float32).reshape(1, 1)
    nw = np.ascontiguousarray(np.asarray(L0_ffn_norm_weight, dtype=np.float32).reshape(NDC, 128).T)
    nwr = np.ascontiguousarray(np.asarray(L0_ffn_norm_weight, dtype=np.float32).reshape(1, D))
    rtrT = np.ascontiguousarray(np.asarray(L0_ffn_router_weight, dtype=np.float32).T)
    wguT = np.ascontiguousarray(np.asarray(L0_ffn_gate_up_weight, dtype=np.float32).transpose(0, 2, 1))
    wdnT = np.ascontiguousarray(np.asarray(L0_ffn_down_weight, dtype=np.float32).transpose(0, 2, 1))

    nc = build_program()
    in_maps = []
    for c in range(NCORES):
        in_maps.append(dict(
            x=x[c], xT=xT[c], cl=cl, cr=cr, thr=thr, gb=gb, alpha=al,
            nw=nw, nwr=nwr, rtrT=rtrT, wguT=wguT, wdnT=wdnT))
    res = run_bass_kernel_spmd(nc, in_maps, core_ids=list(range(NCORES)))
    return np.concatenate([r["out"] for r in res.results], axis=0)


if __name__ == "__main__":
    nc = build_program()
    print("instructions:", nc.n_instructions())


# version 13
# speedup vs baseline: 1.3019x; 1.1814x over previous
import numpy as np
from contextlib import ExitStack

import concourse.bass as bass
import concourse.mybir as mybir
from concourse.bass_utils import run_bass_kernel_spmd

F32 = mybir.dt.float32
BF16 = mybir.dt.bfloat16
AF = mybir.ActivationFunctionType
ALU = mybir.AluOpType
AX = mybir.AxisListType

NCORES = 8
SEQ = 4
T = 2048
NT = 16
D = 1024
NDC = 8
K = 256
KC = 2
HID = 2816
NHB = 22
E = 8
NSW = 2
NTT = 8
LN2C = 0.6931471805599453

DEBUG_STAGE = "full"
PREP_ON = True
TRUNC_QB = 4
CUTS = set()
DUMP = False


def build_program():
    nc = bass.Bass("TRN2", target_bir_lowering=False, debug=False,
                   dynamic_dma_scratch_size=16384,
                   detect_race_conditions=False)

    x_d = nc.dram_tensor("x", [SEQ, T, D], F32, kind="ExternalInput").ap()
    xT_d = nc.dram_tensor("xT", [SEQ, D, T], F32, kind="ExternalInput").ap()
    cl_d = nc.dram_tensor("cl", [D, K], F32, kind="ExternalInput").ap()
    cr_d = nc.dram_tensor("cr", [D, K], F32, kind="ExternalInput").ap()
    thr_d = nc.dram_tensor("thr", [1, 1], F32, kind="ExternalInput").ap()
    gb_d = nc.dram_tensor("gb", [1, 1], F32, kind="ExternalInput").ap()
    al_d = nc.dram_tensor("alpha", [1, 1], F32, kind="ExternalInput").ap()
    nw_d = nc.dram_tensor("nw", [128, NDC], F32, kind="ExternalInput").ap()
    nwr_d = nc.dram_tensor("nwr", [1, D], F32, kind="ExternalInput").ap()
    rtrT_d = nc.dram_tensor("rtrT", [D, E], F32, kind="ExternalInput").ap()
    wguT_d = nc.dram_tensor("wguT", [E, D, 2 * HID], F32, kind="ExternalInput").ap()
    wdnT_d = nc.dram_tensor("wdnT", [E, HID, D], F32, kind="ExternalInput").ap()
    out_d = nc.dram_tensor("out", [SEQ, T, D], F32, kind="ExternalOutput").ap()

    if DUMP:
        d_xfm = nc.dram_tensor("d_xfm", [128, NDC * 2048], BF16, kind="ExternalOutput").ap()
        d_lrt = nc.dram_tensor("d_lrt", [128, 8192], BF16, kind="ExternalOutput").ap()
        d_pt = nc.dram_tensor("d_pt", [128, 8192], BF16, kind="ExternalOutput").ap()
        d_st = nc.dram_tensor("d_st", [128, 64], F32, kind="ExternalOutput").ap()
        d_t0 = nc.dram_tensor("d_t0", [128, 1024], F32, kind="ExternalOutput").ap()
        d_s0 = nc.dram_tensor("d_s0", [128, 16], F32, kind="ExternalOutput").ap()
    sguT = nc.dram_tensor("sguT", [E, NDC, 128, 2 * HID], BF16, kind="Internal").ap()
    sdnT = nc.dram_tensor("sdnT", [E, HID, D], BF16, kind="Internal").ap()
    tokscr = nc.dram_tensor("tokscr", [T, D], BF16, kind="Internal").ap()

    ops = {e: [] for e in ("sp", "pe", "dve", "act", "pool")}

    def em(eng, f):
        ops[eng].append(f)

    cnt = {"sp": 0, "pl": 0, "pe": 0, "dv": 0, "ac": 0, "st": 0}

    with ExitStack() as ctx:
        def sbuf(name, shape, dt):
            return ctx.enter_context(nc.sbuf_tensor(name, shape, dt))

        guard = ctx.enter_context(nc.sbuf_tensor("topguard", [128, 4096], mybir.dt.uint8, side="right"))
        cab = sbuf("cab", [128, NT * E], F32)
        stats = sbuf("stats", [128, 32], F32)
        misc = sbuf("misc", [128, 16], F32)
        nwrow = sbuf("nwrow", [128, 1024], BF16)
        wcols = sbuf("wcols", [128, NDC], F32)
        rtrTf = sbuf("rtrTf", [128, NDC * E], F32)
        rtrTb = sbuf("rtrTb", [128, NDC * E], BF16)
        e8 = sbuf("e8", [128, 4 * E], F32)
        sc_sb = sbuf("sc_sb", [1, 4], F32)
        ones_sb = sbuf("ones_sb", [1, 128], F32)
        sq = sbuf("sq", [128, 64], F32)
        silu_sb = sbuf("silu_sb", [128, 512], BF16)
        silu2_sb = sbuf("silu2_sb", [128, 512], BF16)
        sgex = sbuf("sgex", [128, 1024], BF16)
        stgA = sbuf("stgA", [128, 1408], BF16)
        stream = sbuf("stream", [128, NT * 1024], F32)
        xfm = sbuf("xfm", [128, NDC * 2048], BF16)
        regB = sbuf("regB", [128, NHB * 1024], BF16)
        regC = sbuf("regC", [128, NHB * 1024], BF16)
        regD = sbuf("regD", [128, 4096], BF16)

        PA = ctx.enter_context(nc.psum_tensor("PA", [128, 2048], F32))
        PB = ctx.enter_context(nc.psum_tensor("PB", [128, 2048], F32))

        s_sp = ctx.enter_context(nc.semaphore("s_sp"))
        s_pl = ctx.enter_context(nc.semaphore("s_pl"))
        s_pe = ctx.enter_context(nc.semaphore("s_pe"))
        s_dv = ctx.enter_context(nc.semaphore("s_dv"))
        s_ac = ctx.enter_context(nc.semaphore("s_ac"))
        s_st = ctx.enter_context(nc.semaphore("s_st"))
        SEM = {"sp": s_sp, "pl": s_pl, "pe": s_pe, "dv": s_dv, "ac": s_ac, "st": s_st}

        def dma(eng, out, in_, sem, transpose=False):
            cnt[sem] += 16
            s = SEM[sem]
            em(eng, lambda h, o=out, i=in_, s=s, tr=transpose:
               h.dma_start(out=o, in_=i, transpose=tr).then_inc(s, 16))
            return cnt[sem]

        def wait(eng, sem, v):
            if v <= 0:
                return
            s = SEM[sem]
            em(eng, lambda h, s=s, v=v: h.wait_ge(s, v))

        def bump(sem, n=1):
            cnt[sem] += n
            return cnt[sem]

        def dvg():
            em("dve", lambda h: h.tensor_scalar_mul(misc[0:128, 9:10], misc[0:128, 5:6], 0.0))

        def ac_settle_inc():
            em("act", lambda h: nc.scalar.copy(misc[0:128, 10:11], misc[0:128, 5:6]))
            v = bump("ac")
            em("act", lambda h: nc.scalar.copy(
                misc[0:128, 10:11], misc[0:128, 5:6]).then_inc(s_ac, 1))
            return v

        def dv_settle_inc():
            dvg()
            dvg()
            dvg()
            v = bump("dv")
            em("dve", lambda h: h.tensor_scalar_mul(
                misc[0:128, 9:10], misc[0:128, 5:6], 0.0).then_inc(s_dv, 1))
            return v

        def mm(pslc, lhs, rhs, start, stop, inc=False):
            if inc:
                v = bump("pe")
                em("pe", lambda h, p=pslc, l=lhs, r=rhs, a=start, b=stop:
                   nc.tensor.matmul(p, l, r, start=a, stop=b).then_inc(s_pe, 1))
                return v
            em("pe", lambda h, p=pslc, l=lhs, r=rhs, a=start, b=stop:
               nc.tensor.matmul(p, l, r, start=a, stop=b))
            return None

        THR, GB, ALC, NTHR, GLN2, ZERO, EPS5, EPS6 = range(8)

        def mcol(i):
            return misc[0:128, i:i + 1]

        ones_d = nc.inline_tensor(np.ones((1, 128), dtype=np.float32), "onesrow")
        prep_state = {"last": 0, "war_pe": 0}
        dma("sp", wcols[:, :], nw_d, "sp")
        dma("sp", sc_sb[0:1, 0:1], thr_d, "sp")
        dma("sp", sc_sb[0:1, 1:2], gb_d, "sp")
        dma("sp", sc_sb[0:1, 2:3], al_d, "sp")

        dma("sp", rtrTf[:, :].rearrange("p (dc e) -> p dc e", e=E),
            rtrT_d.rearrange("(dc p) e -> p dc e", p=128), "sp")
        v_pro = dma("sp", ones_sb[0:1, :], ones_d.ap(), "sp")
        v_nwr = dma("pool", stgA[0:1, 0:1024], nwr_d, "pl")
        if "pro" in CUTS:
            prologue_dv = 0
        else:
            wait("pe", "sp", v_pro)
        if "pro" not in CUTS:
            v = mm(PA[:, 0:3], ones_sb[0:1, :], sc_sb[0:1, 0:3], True, True, inc=True)
            wait("dve", "sp", v_pro)
            wait("dve", "pe", v)
            em("dve", lambda h: h.tensor_copy(misc[:, 0:3], PA[:, 0:3]))
            em("dve", lambda h: h.memset(mcol(ZERO), 0.0))
            em("dve", lambda h: h.tensor_scalar(mcol(ALC), mcol(ALC), 2.0, -0.5, ALU.min, ALU.max))
            em("dve", lambda h: h.tensor_scalar_mul(mcol(NTHR), mcol(THR), -1.0))
            em("dve", lambda h: h.tensor_scalar_add(mcol(GLN2), mcol(GB), -LN2C))
            em("dve", lambda h: h.memset(mcol(EPS5), 1e-5))
            em("dve", lambda h: h.memset(misc[0:128, 8:9], 1.0))
            em("dve", lambda h: h.memset(mcol(EPS6), 1e-6))
            em("dve", lambda h: h.memset(stgA[0:1, 1024:1152], 1.0))
            wait("pe", "dv", dv_settle_inc())
            wait("pe", "pl", v_nwr)
            mm(PB[:, 0:512], stgA[0:1, 1024:1152], stgA[0:1, 0:512], True, True)
            v = mm(PB[:, 512:1024], stgA[0:1, 1024:1152], stgA[0:1, 512:1024], True, True, inc=True)
            wait("dve", "pe", v)
            em("dve", lambda h: h.tensor_copy(nwrow[:, :], PB[:, 0:1024]))
            prep_state["war_pe"] = v
            prologue_dv = bump("dv")
            em("dve", lambda h: h.tensor_copy(rtrTb[:, :], rtrTf[:, :]).then_inc(s_dv, 1))
            wait("act", "dv", prologue_dv)

        m_gu = [0] * E
        m_dn = [0] * E
        wguT_r = wguT_d.rearrange("e (dc p) h -> e dc p h", p=128)
        wdnT_r2 = wdnT_d.rearrange("e (c p) d -> e p c d", p=128)
        sdnT_r2 = sdnT.rearrange("e (c p) d -> e p c d", p=128)
        def emit_prep():
            last_store = prep_state["last"]
            wait("pool", "pe", prep_state["war_pe"])
            for e in range(E if PREP_ON else 0):
                for dc in range(NDC):
                    for hf in range(2):
                        for c0 in (0, 1408):
                            wait("pool", "pl", last_store)
                            vc = dma("pool", stgA[:, :],
                                     wguT_r[e, dc, :, hf * HID + c0:hf * HID + c0 + 1408], "pl")
                            wait("pool", "pl", vc)
                            last_store = dma("pool", sguT[e, dc, :, hf * HID + c0:hf * HID + c0 + 1408],
                                             stgA[:, :], "pl")
                m_gu[e] = last_store
                for c2 in range(NHB):
                    wait("pool", "pl", last_store)
                    vc = dma("pool", stgA[:, 0:1024], wdnT_r2[e, :, c2, :], "pl")
                    wait("pool", "pl", vc)
                    last_store = dma("pool", sdnT_r2[e, :, c2, :], stgA[:, 0:1024], "pl")
                m_dn[e] = last_store
            prep_state["last"] = last_store

        x_tok = regB[:, 0:NT * 1024]
        e_sb = regB[:, NT * 1024:NT * 1024 + 2048]
        hidden = regB[:, 0:NHB * 1024]
        LT = regC[:, 0:4096]
        RT = regC[:, 4096:8192]
        CLb = regC[:, 8192:10240]
        CRb = regC[:, 10240:12288]
        PT = regC[:, 12288:20480]
        wd_sb = regC[:, 0:NHB * 1024]

        out_r = out_d.rearrange("s (tc p) d -> s p tc d", p=128)
        x_r = x_d.rearrange("s (tc p) d -> s p tc d", p=128)
        out_rt = out_d.rearrange("s (tc p) d -> s tc p d", p=128)
        x_rt = x_d.rearrange("s (tc p) d -> s tc p d", p=128)
        tok_r = tokscr.rearrange("(tc p) d -> p tc d", p=128)
        stream_r = stream[:, :].rearrange("p (tc d) -> p tc d", d=1024)
        xtok_r = x_tok.rearrange("p (tc d) -> p tc d", d=1024)
        sdnT_r = sdnT.rearrange("e (hc p) d -> e p hc d", p=128)
        sguT_p = sguT.rearrange("e dc p h -> e p dc h")

        pe_ffnA_done = 0
        pe_ffnB_done = 0
        dv_seq_done = prologue_dv
        st_prev = 0
        v_nf_prev = 0

        def ln1(t):
            stt = stream[:, t * 1024:(t + 1) * 1024]
            a = (t % 2) * 16
            ns_c = stats[0:128, a + 0:a + 1]
            nm_c = stats[0:128, a + 1:a + 2]
            vs_c = stats[0:128, a + 2:a + 3]
            sd_c = stats[0:128, a + 3:a + 4]
            rs_c = stats[0:128, a + 4:a + 5]
            vs2_c = stats[0:128, a + 14:a + 15]
            em("dve", lambda h, i=stt, o=ns_c: h.reduce_sum(o, i, axis=AX.X, negate=True))
            dvg()
            v = bump("dv")
            em("dve", lambda h, i=ns_c, o=nm_c: h.tensor_scalar_mul(o, i, 1.0 / D).then_inc(s_dv, 1))
            wait("act", "dv", v)
            em("act", lambda h, i=stt, b=nm_c, ao=vs_c: nc.scalar.activation(
                silu_sb[:, :], i[:, 0:512], AF.Square, bias=b, scale=1.0, accum_out=ao))
            va = bump("ac")
            em("act", lambda h, i=stt, b=nm_c, ao=vs2_c: nc.scalar.activation(
                silu_sb[:, :], i[:, 512:1024], AF.Square, bias=b, scale=1.0, accum_out=ao).then_inc(s_ac, 1))
            wait("dve", "ac", va)
            v = bump("dv")
            em("dve", lambda h, o=vs_c, i=vs_c, j=vs2_c: h.tensor_tensor(o, i, j, ALU.add).then_inc(s_dv, 1))
            dvg()
            wait("act", "dv", v)
            va = bump("ac")
            em("act", lambda h, i=vs_c, o=sd_c: nc.scalar.activation(
                o, i, AF.Sqrt, bias=mcol(EPS5), scale=1.0 / D).then_inc(s_ac, 1))
            wait("dve", "ac", va)
            em("dve", lambda h, i=sd_c, o=rs_c: h.reciprocal(o, i))
            dvg()
            v = bump("dv")
            em("dve", lambda h, io=stt, b=nm_c, r=rs_c: h.tensor_scalar(
                io, io, b, r, ALU.add, ALU.mult).then_inc(s_dv, 1))
            return v

        def rms(t, dv_ready):
            stt = stream[:, t * 1024:(t + 1) * 1024]
            wait("act", "dv", dv_ready)
            a = (t % 2) * 16
            ss_c = stats[0:128, a + 5:a + 6]
            s2_c = stats[0:128, a + 6:a + 7]
            r2_c = stats[0:128, a + 7:a + 8]
            ss2_c = stats[0:128, a + 15:a + 16]
            em("act", lambda h, i=stt, ao=ss_c: nc.scalar.activation(
                silu_sb[:, :], i[:, 0:512], AF.Square, bias=mcol(ZERO), scale=1.0, accum_out=ao))
            va = bump("ac")
            em("act", lambda h, i=stt, ao=ss2_c: nc.scalar.activation(
                silu_sb[:, :], i[:, 512:1024], AF.Square, bias=mcol(ZERO), scale=1.0, accum_out=ao).then_inc(s_ac, 1))
            wait("dve", "ac", va)
            v = bump("dv")
            em("dve", lambda h, o=ss_c, i=ss_c, j=ss2_c: h.tensor_tensor(o, i, j, ALU.add).then_inc(s_dv, 1))
            dvg()
            wait("act", "dv", v)
            va = bump("ac")
            em("act", lambda h, i=ss_c, o=s2_c: nc.scalar.activation(
                o, i, AF.Sqrt, bias=mcol(EPS6), scale=1.0 / D).then_inc(s_ac, 1))
            wait("dve", "ac", va)
            em("dve", lambda h, i=s2_c, o=r2_c: h.reciprocal(o, i))
            dvg()
            yn = x_tok[:, t * 1024:(t + 1) * 1024]
            v = bump("dv")
            em("dve", lambda h, i=stt, r=r2_c, o=yn: h.tensor_scalar_mul(
                o, i, r).then_inc(s_dv, 1))
            return v

        v_xt = [0] * NT
        for t in range(NT):
            v_xt[t] = dma("sp", stream[:, t * 1024:(t + 1) * 1024], x_rt[0, t], "sp")

        for s in range(SEQ):
            v_ct = 0
            v_fm = 0
            if "casts" not in CUTS:
                if "c_xfm" not in CUTS:
                    wait("pool", "pe", pe_ffnA_done)
                    for dc in range(NDC):
                        v_fm = dma("pool", xfm[:, dc * 2048:(dc + 1) * 2048], xT_d[s, dc * 128:(dc + 1) * 128, :], "pl")
                if "c_clcr" not in CUTS:
                    wait("pool", "pe", pe_ffnB_done)
                    dma("pool", CLb.rearrange("p (dc k) -> p dc k", k=K),
                        cl_d.rearrange("(dc p) k -> p dc k", p=128), "pl")
                    v_fm = dma("pool", CRb.rearrange("p (dc k) -> p dc k", k=K),
                               cr_d.rearrange("(dc p) k -> p dc k", p=128), "pl")
                if "c_xtok" not in CUTS:
                    wait("pool", "pl", v_fm)
                    wait("pool", "pe", pe_ffnB_done)
                    v_ct = dma("pool", xtok_r, x_r[s], "pl")
            if s == 0:
                emit_prep()

            if "ltrt" in CUTS:
                attn_evacs_ac = 0
                attn_evacs_dv = 0
                pe_ltrt = 0
            wait("pe", "pl", v_fm)
            wait("pe", "dv", dv_seq_done)
            ac_ev = 0
            for cb, dst, scl, pbank in (() if "ltrt" in CUTS else ((CLb, LT, 1.0 / 32.0, PA), (CRb, RT, 1.0, PB))):
                for kc in range(KC):
                    wait("pe", "dv", ac_ev)
                    vp = None
                    for jg in range(4):
                        pslc = pbank[:, jg * 512:(jg + 1) * 512]
                        for dc in range(NDC):
                            lhs = cb[:, dc * 256 + kc * 128:dc * 256 + kc * 128 + 128]
                            rhs = xfm[:, dc * 2048 + jg * 512:dc * 2048 + (jg + 1) * 512]
                            vp = mm(pslc, lhs, rhs, dc == 0, dc == NDC - 1,
                                    inc=(jg == 3 and dc == NDC - 1))
                    wait("dve", "pe", vp)
                    for jg in range(4):
                        pslc = pbank[:, jg * 512:(jg + 1) * 512]
                        dslc = dst[:, kc * 2048 + jg * 512:kc * 2048 + (jg + 1) * 512]
                        em("dve", lambda h, d=dslc, p=pslc, sc=scl: h.tensor_scalar_mul(d, p, sc))
                    ac_ev = dv_settle_inc()
            if "ltrt" not in CUTS:
                attn_evacs_ac = None
                attn_evacs_dv = ac_ev
                pe_ltrt = cnt["pe"]

            pe_pv_done = 0
            dv_res = [0] * NT
            ac_slot = [0, 0]
            dv_sgex = 0
            ac_prev_stats = 0
            dv_evac_ltrt = attn_evacs_dv
            dv_scale = [0] * NT
            for qb in range(TRUNC_QB):
                dv_pt = 0
                for jc in range(16):
                    pslc = PB[:, 1024 + (jc % 2) * 512:1536 + (jc % 2) * 512]
                    wait("pe", "dv", dv_evac_ltrt)
                    wait("pe", "ac", ac_slot[jc % 2])
                    v_st = None
                    for kc in range(KC):
                        lhs = RT[:, kc * 2048 + jc * 128:kc * 2048 + jc * 128 + 128]
                        rhs = LT[:, kc * 2048 + qb * 512:kc * 2048 + (qb + 1) * 512]
                        v_st = mm(pslc, lhs, rhs, kc == 0, kc == KC - 1, inc=(kc == KC - 1))
                    wait("act", "pe", v_st)
                    wait("act", "dv", dv_sgex)
                    sg = sgex[:, 0:512]
                    ex = sgex[:, 512:1024]
                    em("act", lambda h, o=sg, p=pslc: nc.scalar.sign(o, p, bias=mcol(NTHR)))
                    em("act", lambda h, o=ex, p=pslc: nc.scalar.activation(
                        o, p, AF.Exp, bias=mcol(GLN2), scale=1.0))
                    ac_slot[jc % 2] = ac_settle_inc()
                    wait("dve", "ac", ac_slot[jc % 2])
                    wait("dve", "pe", pe_pv_done)
                    ptp = PT[:, jc * 512:(jc + 1) * 512]
                    dv_sgex = bump("dv")
                    dv_pt = dv_sgex
                    em("dve", lambda h, o=ptp, sg=sg, ex=ex: h.scalar_tensor_tensor(
                        o, sg, 1.0, ex, ALU.add, ALU.mult).then_inc(s_dv, 1))

                    if jc % 4 == 3:
                        t = qb * 4 + jc // 4
                        wait("pe", "ac", ac_prev_stats)
                        v_sc = None
                        for jg in range(4):
                            pslc2 = PA[:, jg * 512:(jg + 1) * 512]
                            for kc in range(KC):
                                lhs = LT[:, kc * 2048 + t * 128:kc * 2048 + t * 128 + 128]
                                rhs = RT[:, kc * 2048 + jg * 512:kc * 2048 + (jg + 1) * 512]
                                v_sc = mm(pslc2, lhs, rhs, kc == 0, kc == KC - 1,
                                          inc=(jg == 3 and kc == KC - 1))
                        a = (t % 2) * 16
                        Zc = stats[0:128, a + 11:a + 12]
                        zr = stats[0:128, a + 13:a + 14]
                        scl_c = e8[0:128, (t % 4):(t % 4) + 1]
                        wait("act", "pe", v_sc)
                        for bk in range(4):
                            em("act", lambda h, bk=bk: nc.scalar.activation(
                                e_sb[:, bk * 512:(bk + 1) * 512],
                                PA[:, bk * 512:(bk + 1) * 512], AF.Exp, bias=mcol(GB),
                                scale=1.0))
                        ac_prev_stats = ac_settle_inc()
                        wait("dve", "ac", ac_prev_stats)
                        em("dve", lambda h, o=Zc: h.reduce_sum(o, e_sb, axis=AX.X))
                        dvg()
                        em("dve", lambda h, o=zr, i=Zc: h.reciprocal(o, i))
                        dvg()
                        dv_scale[t] = bump("dv")
                        em("dve", lambda h, o=scl_c, i=zr: h.tensor_scalar_mul(
                            o, i, 1.0).then_inc(s_dv, 1))

                for ti in range(4):
                    t = qb * 4 + ti
                    wait("pe", "dv", dv_pt)
                    wait("pe", "pl", v_ct)
                    if t > 0:
                        wait("pe", "dv", dv_res[t - 1])
                    v_pv = None
                    for jc in range(16):
                        lhs = PT[:, jc * 512 + ti * 128:jc * 512 + ti * 128 + 128]
                        for hf in range(2):
                            rhs = x_tok[:, jc * 1024 + hf * 512:jc * 1024 + (hf + 1) * 512]
                            pslc = PB[:, hf * 512:(hf + 1) * 512]
                            v_pv = mm(pslc, lhs, rhs, jc == 0, jc == 15,
                                      inc=(jc == 15 and hf == 1))
                    pe_pv_done = v_pv
                    wait("dve", "pe", pe_pv_done)
                    wait("dve", "sp", v_xt[t])
                    scl_c = e8[0:128, (t % 4):(t % 4) + 1]
                    if "fixscale" in CUTS:
                        scl_c = misc[0:128, 8:9]
                    stt = stream[:, t * 1024:(t + 1) * 1024]
                    v = bump("dv")
                    em("dve", lambda h, o=stt, sc=scl_c: h.scalar_tensor_tensor(
                        o, PB[:, 0:1024], sc, o, ALU.mult, ALU.add).then_inc(s_dv, 1))
                    if DUMP and s == 0 and t == 0:
                        wait("sp", "dv", v)
                        dma("sp", d_t0, stt, "st")
                        dma("sp", d_s0[:, 0:6], stats[0:128, 8:14], "st")
                        dma("sp", d_s0[:, 8:12], e8[0:128, 0:4], "st")
                    dv_res[t] = ln1(t)

            if DEBUG_STAGE == "attn":
                wait("sp", "dv", dv_res[TRUNC_QB * 4 - 1] if TRUNC_QB else 0)
                wait("sp", "sp", v_xt[NT - 1])
                st_prev = dma("sp", out_r[s], stream_r, "st")
                if DUMP and s == 0:
                    dma("sp", d_xfm, xfm[:, :], "st")
                    dma("sp", d_lrt, regC[:, 0:8192], "st")
                    dma("sp", d_pt, PT[:, :], "st")
                    st_prev = dma("sp", d_st[:, 32:36], e8[:, 0:4], "st")
                wait("sp", "st", st_prev)
                if s + 1 < SEQ:
                    for t in range(NT):
                        v_xt[t] = dma("sp", stream[:, t * 1024:(t + 1) * 1024],
                                      x_rt[s + 1, t], "sp")
                pe_ffnA_done = cnt["pe"]
                pe_ffnB_done = cnt["pe"]
                dv_seq_done = cnt["dv"]
                continue

            wait("sp", "sp", v_nf_prev)
            st_tok = 0
            dv_yn_t = [0] * NT
            for t in range(NT):
                dv_yn_t[t] = rms(t, dv_res[t])
                stt2 = stream[:, t * 1024:(t + 1) * 1024]
                em("dve", lambda h, io=stt2: h.tensor_scalar_mul(io, io, 2.0))
                if t >= 1:
                    wait("sp", "dv", dv_yn_t[t - 1])
                    st_tok = dma("sp", tok_r[:, t - 1, :], xtok_r[:, t - 1, :], "sp")
            wait("sp", "dv", dv_settle_inc())
            st_tok = dma("sp", tok_r[:, NT - 1, :], xtok_r[:, NT - 1, :], "sp")
            wait("sp", "sp", st_tok)
            wait("sp", "pe", pe_ltrt)
            v_nf = 0
            for dc in range(NDC):
                v_nf = dma("sp", xfm[:, dc * 2048:(dc + 1) * 2048],
                           tokscr[:, dc * 128:(dc + 1) * 128], "sp", transpose=True)
            v_nf_prev = v_nf
            wait("dve", "sp", v_nf)
            for dc in range(NDC):
                slc = xfm[:, dc * 2048:(dc + 1) * 2048]
                em("dve", lambda h, io=slc, w=wcols[0:128, dc:dc + 1]:
                   h.tensor_scalar_mul(io, io, w))
            dv_wsc = dv_settle_inc()

            wait("pe", "dv", dv_wsc)
            router_ac = [0, 0, 0, 0]
            for t in range(NT):
                pslc = PA[:, (t % 4) * 512:(t % 4) * 512 + 8]
                wait("pe", "ac", router_ac[t % 4])
                v = None
                for dc in range(NDC):
                    lhs = xfm[:, dc * 2048 + t * 128:dc * 2048 + t * 128 + 128]
                    rhs = rtrTb[:, dc * E:(dc + 1) * E]
                    v = mm(pslc, lhs, rhs, dc == 0, dc == NDC - 1, inc=(dc == NDC - 1))
                q = (t % 4) * E
                e8s = e8[:, q:q + E]
                a = (t % 2) * 16
                nm8 = stats[0:128, a + 8:a + 9]
                m1 = stats[0:128, a + 9:a + 10]
                m2 = stats[0:128, a + 10:a + 11]
                den = stats[0:128, a + 11:a + 12]
                wiv = stats[0:128, a + 12:a + 13]
                wa = stats[0:128, a + 13:a + 14]
                wb = stats[0:128, a + 15:a + 16]
                wait("dve", "pe", v)
                v = bump("dv")
                em("dve", lambda h, o=nm8, p=pslc: h.reduce_max(
                    o, p, axis=AX.X, negate=True).then_inc(s_dv, 1))
                wait("act", "dv", v)
                va = bump("ac")
                router_ac[t % 4] = va
                em("act", lambda h, o=e8s, p=pslc, b=nm8: nc.scalar.activation(
                    o, p, AF.Exp, bias=b, scale=1.0).then_inc(s_ac, 1))
                wait("dve", "ac", va)
                eq1 = sq[:, 0:8]
                tneg = sq[:, 8:16]
                pm = sq[:, 16:24]
                eq2 = sq[:, 24:32]
                c1 = sq[:, 32:40]
                em("dve", lambda h, o=m1, i=e8s: h.reduce_max(o, i, axis=AX.X))
                dvg()
                em("dve", lambda h, o=eq1, i=e8s, m=m1: h.tensor_single_scalar(o, i, m, ALU.is_equal))
                dvg()
                em("dve", lambda h, o=tneg, i=eq1: h.tensor_scalar(o, i, -1.0, 1.0, ALU.mult, ALU.add))
                dvg()
                em("dve", lambda h, o=pm, i=e8s, tn=tneg: h.tensor_tensor(o, i, tn, ALU.mult))
                dvg()
                em("dve", lambda h, o=m2, i=pm: h.reduce_max(o, i, axis=AX.X))
                dvg()
                em("dve", lambda h, o=den, i=m1, j=m2: h.tensor_tensor(o, i, j, ALU.add))
                dvg()
                em("dve", lambda h, o=wiv, i=den: h.reciprocal(o, i))
                dvg()
                em("dve", lambda h, o=wiv, i=wiv: h.tensor_tensor(o, i, mcol(ALC), ALU.mult))
                dvg()
                em("dve", lambda h, o=wa, i=m1, w=wiv: h.tensor_tensor(o, i, w, ALU.mult))
                em("dve", lambda h, o=wb, i=m2, w=wiv: h.tensor_tensor(o, i, w, ALU.mult))
                dvg()
                em("dve", lambda h, o=eq2, i=pm, m=m2: h.tensor_single_scalar(o, i, m, ALU.is_equal))
                em("dve", lambda h, o=c1, i=eq1, w=wa: h.tensor_single_scalar(o, i, w, ALU.mult))
                dvg()
                cslc = cab[:, t * E:(t + 1) * E]
                dv_rtr = bump("dv")
                em("dve", lambda h, o=cslc, i=eq2, w=wb, c=c1: h.scalar_tensor_tensor(
                    o, i, w, c, ALU.mult, ALU.add).then_inc(s_dv, 1))

            dv_stt_dn = 0
            pe_wd_free = 0
            dv_evac = [0, 0]
            pe_lastB = 0
            sched = [(e_, sw_, c_) for e_ in range(E) for sw_ in range(NSW) for c_ in range(NHB)]
            v_set = {}
            ptr = [0]

            def load_set(war_pe):
                if ptr[0] >= len(sched):
                    return
                e_, sw_, c_ = sched[ptr[0]]
                ptr[0] += 1
                buf = (sw_ * NHB + c_) % 2
                wait("sp", "pl", m_gu[e_])
                wait("sp", "pe", war_pe)
                gbuf = regD[:, buf * 2048:buf * 2048 + 1024]
                ubuf = regD[:, buf * 2048 + 1024:buf * 2048 + 2048]
                dma("sp", gbuf.rearrange("p (dc h) -> p dc h", h=128),
                    sguT_p[e_, :, :, c_ * 128:(c_ + 1) * 128], "sp")
                v_set[(e_, sw_, c_)] = dma("sp", ubuf.rearrange("p (dc h) -> p dc h", h=128),
                                           sguT_p[e_, :, :, HID + c_ * 128:HID + (c_ + 1) * 128], "sp")

            load_set(pe_ffnB_done)
            load_set(pe_ffnB_done)
            for e in range(E):
                wait("sp", "pl", m_dn[e])
                wait("sp", "pe", pe_wd_free)
                v_wd = dma("sp", wd_sb.rearrange("p (hc d) -> p hc d", d=D), sdnT_r[e], "sp")
                for sw in range(NSW):
                    for c in range(NHB):
                        buf = (sw * NHB + c) % 2
                        par = c % 2
                        pb = PA if par == 0 else PB
                        gp = pb[:, 0:1024]
                        up = pb[:, 1024:2048]
                        wait("pe", "sp", v_set[(e, sw, c)])
                        wait("pe", "dv", dv_evac[par])
                        if c < 2:
                            wait("pe", "dv", dv_stt_dn)
                            if e == 0 and sw == 0:
                                wait("pe", "dv", dv_rtr)
                                wait("pe", "ac", max(router_ac))
                        v_pe = None
                        for gu_i, pslc in ((0, gp), (1, up)):
                            base = buf * 2048 + gu_i * 1024
                            for dc in range(NDC):
                                lhs = regD[:, base + dc * 128:base + dc * 128 + 128]
                                for hf in range(2):
                                    rhs = xfm[:, dc * 2048 + sw * 1024 + hf * 512:
                                              dc * 2048 + sw * 1024 + (hf + 1) * 512]
                                    v_pe = mm(pslc[:, hf * 512:(hf + 1) * 512], lhs, rhs,
                                              dc == 0, dc == NDC - 1,
                                              inc=(gu_i == 1 and dc == NDC - 1 and hf == 1))
                        load_set(v_pe)
                        wait("act", "pe", v_pe)
                        wait("act", "dv", max(dv_evac))
                        em("act", lambda h, g=gp: nc.scalar.activation(
                            silu_sb[:, :], g[:, 0:512], AF.Silu, bias=mcol(ZERO), scale=1.0))
                        em("act", lambda h, g=gp: nc.scalar.activation(
                            silu2_sb[:, :], g[:, 512:1024], AF.Silu, bias=mcol(ZERO), scale=1.0))
                        va = ac_settle_inc()
                        wait("dve", "ac", va)
                        hslc = hidden[:, c * 1024:(c + 1) * 1024]
                        em("dve", lambda h, o=hslc, u=up: h.tensor_tensor(
                            o[:, 0:512], silu_sb[:, :], u[:, 0:512], ALU.mult))
                        dv_evac[par] = bump("dv")
                        em("dve", lambda h, o=hslc, u=up: h.tensor_tensor(
                            o[:, 512:1024], silu2_sb[:, :], u[:, 512:1024], ALU.mult).then_inc(s_dv, 1))
                    dv_hid_all = max(dv_evac)
                    wait("pe", "sp", v_wd)
                    for tt in range(NTT):
                        t = sw * NTT + tt
                        pb = PA if tt % 2 == 0 else PB
                        dp = pb[:, 0:1024]
                        wait("pe", "dv", dv_hid_all if tt == 0 else 0)
                        wait("pe", "dv", dv_stt_dn if tt >= 2 else 0)
                        v_pe = None
                        for hc in range(NHB):
                            lhs = hidden[:, hc * 1024 + tt * 128:hc * 1024 + tt * 128 + 128]
                            for hf in range(2):
                                rhs = wd_sb[:, hc * 1024 + hf * 512:hc * 1024 + (hf + 1) * 512]
                                v_pe = mm(dp[:, hf * 512:(hf + 1) * 512], lhs, rhs,
                                          hc == 0, hc == NHB - 1,
                                          inc=(hc == NHB - 1 and hf == 1))
                        pe_lastB = v_pe
                        wait("dve", "pe", v_pe)
                        stt = stream[:, t * 1024:(t + 1) * 1024]
                        ccol = cab[0:128, t * E + e:t * E + e + 1]
                        dv_stt_dn = bump("dv")
                        em("dve", lambda h, o=stt, d=dp, c=ccol: h.scalar_tensor_tensor(
                            o, d, c, o, ALU.mult, ALU.add).then_inc(s_dv, 1))
                    dv_evac = [dv_stt_dn, dv_stt_dn]
                pe_wd_free = pe_lastB
            pe_ffnA_done = cnt["pe"]
            pe_ffnB_done = cnt["pe"]

            dv_seq_done = dv_stt_dn
            nxt = [0] * NT
            dv_ln = [0] * NT
            for t in range(NT):
                dv_ln[t] = ln1(t)
            wait("act", "dv", dv_settle_inc())
            for t in range(NT):
                wait("act", "dv", dv_ln[t])
                st_t = dma("act", out_rt[s, t], stream[:, t * 1024:(t + 1) * 1024], "st")
                if s + 1 < SEQ:
                    wait("sp", "st", st_t)
                    nxt[t] = dma("sp", stream[:, t * 1024:(t + 1) * 1024],
                                 x_rt[s + 1, t], "sp")
            v_xt = nxt

        wait("sp", "st", cnt["st"])

        with nc.Block() as block:
            @block.sync
            def _(h):
                for f in ops["sp"]:
                    f(h)

            @block.tensor
            def _(h):
                for f in ops["pe"]:
                    f(h)

            @block.vector
            def _(h):
                for f in ops["dve"]:
                    f(h)

            @block.scalar
            def _(h):
                for f in ops["act"]:
                    f(h)

            @block.gpsimd
            def _(h):
                for f in ops["pool"]:
                    f(h)

    return nc


def kernel(L0_attn_x, L0_attn_compare_left, L0_attn_compare_right,
           L0_attn_threshold, L0_attn_gate_bias, L0_ffn_norm_weight,
           L0_ffn_router_weight, L0_ffn_gate_up_weight, L0_ffn_down_weight,
           L0_ffn_residual_alpha):
    x = np.ascontiguousarray(np.asarray(L0_attn_x, dtype=np.float32))
    xT = np.ascontiguousarray(x.transpose(0, 2, 1).reshape(NCORES, SEQ, D, T))
    x = x.reshape(NCORES, SEQ, T, D)
    cl = np.ascontiguousarray(np.asarray(L0_attn_compare_left, dtype=np.float32))
    cr = np.ascontiguousarray(np.asarray(L0_attn_compare_right, dtype=np.float32))
    thr = np.asarray(L0_attn_threshold, dtype=np.float32).reshape(1, 1)
    gb = np.asarray(L0_attn_gate_bias, dtype=np.float32).reshape(1, 1)
    al = np.asarray(L0_ffn_residual_alpha, dtype=np.float32).reshape(1, 1)
    nw = np.ascontiguousarray(np.asarray(L0_ffn_norm_weight, dtype=np.float32).reshape(NDC, 128).T)
    nwr = np.ascontiguousarray(np.asarray(L0_ffn_norm_weight, dtype=np.float32).reshape(1, D))
    rtrT = np.ascontiguousarray(np.asarray(L0_ffn_router_weight, dtype=np.float32).T)
    wguT = np.ascontiguousarray(np.asarray(L0_ffn_gate_up_weight, dtype=np.float32).transpose(0, 2, 1))
    wdnT = np.ascontiguousarray(np.asarray(L0_ffn_down_weight, dtype=np.float32).transpose(0, 2, 1))

    nc = build_program()
    in_maps = []
    for c in range(NCORES):
        in_maps.append(dict(
            x=x[c], xT=xT[c], cl=cl, cr=cr, thr=thr, gb=gb, alpha=al,
            nw=nw, nwr=nwr, rtrT=rtrT, wguT=wguT, wdnT=wdnT))
    res = run_bass_kernel_spmd(nc, in_maps, core_ids=list(range(NCORES)))
    return np.concatenate([r["out"] for r in res.results], axis=0)


if __name__ == "__main__":
    nc = build_program()
    print("instructions:", nc.n_instructions())
